# version 08
# baseline (speedup 1.0000x reference)
import numpy as np
import ml_dtypes
import concourse.bass as bass
import concourse.mybir as mybir
from concourse.bass import ds
from concourse.bass_utils import run_bass_kernel_spmd

F32 = mybir.dt.float32
BF16 = mybir.dt.bfloat16
I32 = mybir.dt.int32
U32 = mybir.dt.uint32
AF = mybir.ActivationFunctionType
ALU = mybir.AluOpType
AX = mybir.AxisListType
BF = ml_dtypes.bfloat16

D, S, NH, DKV = 2048, 4096, 16, 128
NE, NI, SI = 16, 1408, 2816
ICN = NI // 128
NDC = D // 128
KB = 1024

_DEBUG = False
_RACEDET = False
_STAGES = 99
_SUB = "lkvqrP"
_NPANEL = 16



def _stage(inp):
    x = np.asarray(inp["L0_attn_x"], dtype=np.float32)
    perm = np.concatenate([np.arange(0, 128, 2), np.arange(1, 128, 2)])

    Wq, Wk, Wv, Wo = (np.asarray(inp[f"L0_attn_W{p}"], np.float32) for p in "qkvo")
    Aq, Ak, Av, Ao = (np.asarray(inp[f"L0_attn_A{p}"], np.float32) for p in "qkvo")
    Bq, Bk, Bv, Bo = (np.asarray(inp[f"L0_attn_B{p}"], np.float32) for p in "qkvo")

    wqkvH, lb = {}, {}
    for G in (0, 1):
        mats, lbs = [], {}
        for p, W, Bm, permute in (("q", Wq, Bq, True), ("k", Wk, Bk, True),
                                  ("v", Wv, Bv, False)):
            rows = np.concatenate([h * 128 + (perm if permute else np.arange(128))
                                   for h in range(8 * G, 8 * G + 8)])
            mats.append(np.ascontiguousarray(W[rows].T).reshape(NDC, 128, 1024))
            lbs[p] = np.ascontiguousarray(
                Bm[:, rows, :].transpose(0, 2, 1).reshape(64, 1024))
        arr = np.stack(mats)
        arr = arr.reshape(3, NDC, 128, 2, 512).transpose(3, 2, 0, 1, 4)
        wqkvH[G] = np.ascontiguousarray(arr.reshape(2, 128, 3 * NDC * 512))
        lb[G] = lbs
    lbo = np.ascontiguousarray(Bo.transpose(0, 2, 1).reshape(64, 2048))

    woH = np.ascontiguousarray(
        Wo.T.reshape(NDC, 128, 2048).transpose(1, 0, 2).reshape(128, NDC * 2048))
    laqkvH = np.ascontiguousarray(
        np.stack([A.transpose(2, 0, 1).reshape(D, 64).reshape(NDC, 128, 64)
                  for A in (Aq, Ak, Av)])
        .transpose(2, 0, 1, 3).reshape(128, 3 * NDC * 64))
    laoH = np.ascontiguousarray(
        Ao.transpose(2, 0, 1).reshape(D, 64).reshape(NDC, 128, 64)
        .transpose(1, 0, 2).reshape(128, NDC * 64))
    gate = np.asarray(inp["L0_ffn_gate_w"], np.float32)
    gateH = np.ascontiguousarray(gate.T.reshape(NDC, 128, 16)
                                 .transpose(1, 0, 2).reshape(128, NDC * 16))

    xPH = {}
    for b in (0, 1):
        xt = np.ascontiguousarray(x[b].T).reshape(NDC, 128, 16, 256)
        xPH[b] = np.ascontiguousarray(
            xt.transpose(2, 1, 0, 3).reshape(16 * 128, NDC * 256))

    inv = 10000.0 ** (-np.arange(0, 128, 2, dtype=np.float64) / 128.0)
    angK = np.arange(S, dtype=np.float64)[None, :] * inv[:, None]
    ropekC = np.concatenate([np.cos(angK), np.cos(angK)], 0).astype(BF)
    ropekS = np.concatenate([-np.sin(angK), np.sin(angK)], 0).astype(BF)
    sc = 1.0 / np.sqrt(128.0)
    tri = np.where(np.arange(128)[None, :] < np.arange(128)[:, None], -1e9, 0.0)
    full = np.full((128, 128), -1e9)
    zero = np.zeros((128, 128))
    ropeqH, masksH = {}, {}
    for H in (0, 1):
        pq = np.concatenate([np.arange((2 * i + H) * 256, (2 * i + H) * 256 + 256)
                             for i in range(8)]).astype(np.float64)
        angQ = pq[None, :] * inv[:, None]
        ropeqH[H] = (np.concatenate([np.cos(angQ) * sc, np.cos(angQ) * sc], 0).astype(BF),
                     np.concatenate([-np.sin(angQ) * sc, np.sin(angQ) * sc], 0).astype(BF))
        m = np.zeros((16, 3, 128, 128), np.float32)
        for qc in range(16):
            m[qc] = (tri, full, full) if H == 0 else (zero, zero, tri)
        masksH[H] = np.ascontiguousarray(
            m.transpose(2, 0, 1, 3).reshape(128, 16 * 3 * 128)).astype(BF)

    w1 = np.asarray(inp["L0_ffn_w1"], np.float32)
    w3 = np.asarray(inp["L0_ffn_w3"], np.float32)
    w2 = np.asarray(inp["L0_ffn_w2"], np.float32)
    sw1 = np.asarray(inp["L0_ffn_sw1"], np.float32)
    sw3 = np.asarray(inp["L0_ffn_sw3"], np.float32)
    sw2 = np.asarray(inp["L0_ffn_sw2"], np.float32)

    def tile13(w):
        e = w.shape[0]
        icn = w.shape[1] // 128
        arr = w.reshape(e, icn, 128, NDC, 128).transpose(0, 1, 4, 3, 2)
        return np.ascontiguousarray(arr.reshape(e, icn, 128, D))

    def tile2(w):
        e = w.shape[0]
        icn = w.shape[2] // 128
        return np.ascontiguousarray(w.transpose(0, 2, 1).reshape(e, icn, 128, D))

    w1H, w3H, w2H = tile13(w1), tile13(w3), tile2(w2)
    sw1H = tile13(sw1[None]).reshape(2, ICN, 128, D)
    sw3H = tile13(sw3[None]).reshape(2, ICN, 128, D)
    sw2H = tile2(sw2[None]).reshape(2, ICN, 128, D)

    starts = np.asarray(inp["L0_attn_lora_start_indices"], np.int32).reshape(1, 4)
    arr64 = (np.arange(64) // 16).astype(np.float32).reshape(64, 1)
    nonce = np.array([[np.random.randint(1, 2 ** 31)]], dtype=np.uint32)

    maps = []
    for c in range(8):
        B, H, G = c // 4, (c // 2) % 2, c % 2
        xQH = np.ascontiguousarray(np.concatenate(
            [xPH[B][(2 * gi + H) * 128:(2 * gi + H) * 128 + 128] for gi in range(8)], 0))
        blocks = [4 * j + 2 * G + H for j in range(4)]
        xres = np.ascontiguousarray(
            np.concatenate([x[B, b * 256:(b + 1) * 256] for b in blocks], 0)
            .reshape(8, 128, 2048))
        maps.append({
            "xPH": xPH[B], "xQH": xQH, "xres": xres, "wqkv": wqkvH[G], "wo": woH,
            "laqkv": laqkvH, "lao": laoH,
            "lbq": lb[G]["q"], "lbk": lb[G]["k"], "lbv": lb[G]["v"], "lbo": lbo,
            "gateT": gateH, "ropekC": ropekC, "ropekS": ropekS,
            "ropeqC": ropeqH[H][0], "ropeqS": ropeqH[H][1],
            "masks": masksH[H], "starts": starts,
            "bval": np.array([[B + 0.5]], np.float32), "arr64": arr64,
            "nonce": nonce,
            "w1T": w1H, "w3T": w3H, "w2T": w2H,
            "sw1T": sw1H, "sw3T": sw3H, "sw2T": sw2H,
        })
        if _STAGES < 4:
            for k in ("w1T", "w3T", "w2T", "sw1T", "sw3T", "sw2T"):
                maps[-1].pop(k)
    return maps


def _assemble(results):
    out = np.empty((2, S, D), np.float32)
    for c in range(8):
        B, H, G = c // 4, (c // 2) % 2, c % 2
        o = np.asarray(results[c]["out"])
        for j in range(4):
            gb = 4 * j + 2 * G + H
            out[B, gb * 256:(gb + 1) * 256] = o[2 * j:2 * j + 2].reshape(256, 2048)
    return out



class _Pl:
    ENGS = ("s", "g", "t", "v", "a")
    ATTR = {"s": "sync", "g": "gpsimd", "t": "tensor", "v": "vector", "a": "scalar"}

    def __init__(self, nc):
        self.nc = nc
        self.q = {e: [] for e in self.ENGS}
        self.sem = {}
        for name in ("te", "ve", "ae", "ge"):
            self.sem[name] = [nc.alloc_semaphore(name=name), 0]
        self.hw = {e: {} for e in self.ENGS}
        self.war = {}

    def _getsem(self, name):
        if name not in self.sem:
            self.sem[name] = [self.nc.alloc_semaphore(name=name), 0]
        return self.sem[name]

    def _wait(self, eng, toks):
        byname = {}
        for t in toks:
            if t is None:
                continue
            sn, v = t
            if v > byname.get(sn, 0):
                byname[sn] = v
        for sn, v in byname.items():
            if eng == "t" and sn == "te":
                continue
            if self.hw[eng].get(sn, 0) >= v:
                continue
            self.hw[eng][sn] = v
            h = self._getsem(sn)[0]
            self.q[eng].append(lambda e, h=h, v=v: e.wait_ge(h, v))

    def emit(self, eng, fn, after=(), inc=None, dma=False):
        self._wait(eng, after)
        tok = None
        if inc is not None:
            rec = self._getsem(inc)
            rec[1] += 16 if dma else 1
            h, tgt, k = rec[0], rec[1], (16 if dma else 1)
            self.q[eng].append(lambda e, fn=fn, h=h, k=k: fn(e).then_inc(h, k))
            tok = (inc, tgt)
        else:
            self.q[eng].append(fn)
        return tok

    def dma(self, eng, out, in_, after=(), stream="misc"):
        return self.emit(eng, lambda e, o=out, i=in_: e.dma_start(out=o, in_=i),
                         after=after, inc=stream, dma=True)

    def mm(self, out, lhsT, rhs, start, stop, after=(), inc=False):
        return self.emit(
            "t", lambda e, o=out, l=lhsT, r=rhs, a=start, b=stop:
            e.matmul(o, l, r, start=a, stop=b),
            after=after, inc=("te" if inc else None))

    def use(self, *names):
        out = []
        for n in names:
            v = self.war.get(n)
            if v is None:
                continue
            out.extend(v if isinstance(v, list) else [v])
        return [t for t in out if t is not None]

    def setw(self, name, tok):
        if tok is not None:
            self.war[name] = tok

    def run(self, block):
        for name in self.ENGS:
            fns = self.q[name]
            if not fns:
                continue
            getattr(block, self.ATTR[name])(
                lambda eng, fns=fns: [f(eng) for f in fns])



def _build():
    nc = bass.Bass("TRN2", target_bir_lowering=False, debug=False,
                   detect_race_conditions=_RACEDET)

    def di(name, shape, dt=F32):
        return nc.dram_tensor(name, list(shape), dt, kind="ExternalInput").ap()

    io = {}
    io["xPH"] = di("xPH", [16 * 128, NDC * 256])
    io["xQH"] = di("xQH", [8 * 128, NDC * 256])
    io["xres"] = di("xres", [8, 128, 2048])
    io["wqkv"] = di("wqkv", [2, 128, 3 * NDC * 512])
    io["wo"] = di("wo", [128, NDC * 2048])
    io["laqkv"] = di("laqkv", [128, 3 * NDC * 64])
    io["lao"] = di("lao", [128, NDC * 64])
    for n in ("lbq", "lbk", "lbv"):
        io[n] = di(n, [64, 1024])
    io["lbo"] = di("lbo", [64, 2048])
    io["gateT"] = di("gateT", [128, NDC * 16])
    io["ropekC"] = di("ropekC", [128, S], BF16)
    io["ropekS"] = di("ropekS", [128, S], BF16)
    io["ropeqC"] = di("ropeqC", [128, 2048], BF16)
    io["ropeqS"] = di("ropeqS", [128, 2048], BF16)
    io["masks"] = di("masks", [128, 16 * 3 * 128], BF16)
    io["starts"] = di("starts", [1, 4], I32)
    io["bval"] = di("bval", [1, 1])
    io["arr64"] = di("arr64", [64, 1])
    io["nonce"] = di("nonce", [1, 1], U32)
    if _STAGES >= 4:
        io["w1T"] = di("w1T", [NE, ICN, 128, D])
        io["w3T"] = di("w3T", [NE, ICN, 128, D])
        io["w2T"] = di("w2T", [NE, ICN, 128, D])
        io["sw1T"] = di("sw1T", [2, ICN, 128, D])
        io["sw3T"] = di("sw3T", [2, ICN, 128, D])
        io["sw2T"] = di("sw2T", [2, ICN, 128, D])
    io["out"] = nc.dram_tensor("out", [8, 128, 2048], F32,
                               kind="ExternalOutput").ap()
    io["shp"] = nc.dram_tensor("shp", [2, 2048, 1024], BF16, kind="Internal",
                               addr_space="Shared").ap()
    io["shpf"] = io["shp"].rearrange("s q c -> (s q) c")
    io["shf"] = nc.dram_tensor("shf", [2, 64], U32, kind="Internal",
                               addr_space="Shared").ap()
    if _DEBUG:
        io["dbgq"] = nc.dram_tensor("dbgq", [128, 8192], BF16, kind="ExternalOutput").ap()
        io["dbgk"] = nc.dram_tensor("dbgk", [128, 4 * S], BF16, kind="ExternalOutput").ap()
        io["dbgv"] = nc.dram_tensor("dbgv", [128, 32 * 516], BF16, kind="ExternalOutput").ap()
        io["dbgattn"] = nc.dram_tensor("dbgattn", [8, 128, 2048], BF16, kind="ExternalOutput").ap()
        io["dbgx1"] = nc.dram_tensor("dbgx1", [8, 128, 2048], F32, kind="ExternalOutput").ap()
        io["dbggw"] = nc.dram_tensor("dbggw", [128, 128], F32, kind="ExternalOutput").ap()
        io["dbgh"] = nc.dram_tensor("dbgh", [128, ICN * 1024], BF16, kind="ExternalOutput").ap()

    sb = {}

    def ps_(name, shape, dt=F32):
        sb[name] = nc.alloc_psum_tensor(name, list(shape), dt)

    def pers(name, shape, dt):
        sb[name] = nc.alloc_sbuf_tensor(name, list(shape), dt)

    pers("ident", [128, 128], BF16)
    pers("ones_bf", [128, 1], BF16)
    pers("ones_f1", [1, 128], F32)
    pers("lao_sb", [128, NDC * 64], BF16)
    pers("lbo_sb", [64, 2048], BF16)
    pers("gate_sb", [128, NDC * 16], BF16)
    pers("wg_sb", [128, 8 * 16], F32)
    pers("gsc", [128, 48], F32)
    pers("sel_sb", [64, 1], F32)
    pers("arr64_sb", [64, 1], F32)
    pers("starts_f", [1, 8], F32)
    pers("starts_i", [1, 8], I32)
    pers("bval_sb", [1, 1], F32)
    pers("nonce_sb", [1, 64], U32)
    pers("flag_sb", [1, 64], U32)
    pers("tiny", [128, 16], F32)
    pers("t11", [1, 16], F32)
    pers("bias_h", [128, 1], F32)
    pers("epsb", [128, 1], F32)
    pers("rstd", [128, 4], F32)
    pers("ssum", [128, 4], F32)

    base = (nc.sbuf_base + 63) & ~63
    AR = 197 * KB
    assert base + AR <= nc.sbuf_top, (base, nc.sbuf_top, nc.sbuf_top - base)

    def at(name, off_kb, shape, dt):
        sb[name] = nc.alloc_sbuf_tensor_at(name, list(shape), dt,
                                           offset=base + int(off_kb * KB))

    at("b_xg0", 0, [128, NDC * 256], BF16)
    at("b_xg1", 8, [128, NDC * 256], BF16)
    at("b_xq", 16, [128, NDC * 256], BF16)
    at("b_low0", 24, [128, 1024], BF16)
    at("b_low1", 26, [128, 1024], BF16)
    at("b_qfm", 28, [128, 4 * 2048], BF16)
    at("b_kfm", 44, [128, 4 * S], BF16)
    at("b_v", 76, [128, 32 * 516], BF16)
    at("b_wqkv", 109, [128, 3 * NDC * 512], BF16)
    at("b_rkC", 157, [128, S], BF16)
    at("b_rkS", 165, [128, S], BF16)
    at("b_rqC", 173, [128, 2048], BF16)
    at("b_rqS", 177, [128, 2048], BF16)
    at("b_laqkv", 181, [128, 3 * NDC * 64], BF16)
    at("b_lb", 187, [64, 3 * 1024], BF16)
    at("b_us", 193, [128, 512], BF16)
    at("b_m1", 194, [128, 512], BF16)
    at("b_m2", 195, [128, 512], BF16)
    at("b_uu", 196, [128, 512], BF16)
    at("b_pt0", 109, [128, 1024], BF16)
    at("b_pt1", 111, [128, 1024], BF16)
    at("b_pt2", 113, [128, 1024], BF16)
    at("b_pt3", 115, [128, 1024], BF16)
    at("b_masks", 117, [128, 16 * 3 * 128], BF16)
    at("b_qsq", 129, [128, S], BF16)
    at("b_astg", 137, [128, 4 * 2048], BF16)
    at("d_wo", 0, [128, NDC * 2048], BF16)
    at("d_attnP", 64, [128, 4 * 2048], BF16)
    at("d_afm", 80, [128, 2048], BF16)
    at("d_xres", 84, [128, 2048], F32)
    at("d_lowo", 92, [64, 1024], BF16)
    at("d_sq", 0, [128, 2048], BF16)
    at("d_x1tm", 8, [128, 2048], BF16)
    at("x1fm", 96, [128, NDC * 1024], BF16)
    at("res", 128, [128, 8 * 2048], F32)
    at("e_h", 0, [128, ICN * 1024], BF16)
    at("e_w1r0", 22, [128, D], BF16)
    at("e_w1r1", 26, [128, D], BF16)
    at("e_w3r0", 30, [128, D], BF16)
    at("e_w3r1", 34, [128, D], BF16)
    at("e_w2", 38, [128, ICN * 2048], BF16)
    at("e_t1", 82, [128, 1024], BF16)
    at("e_oscr", 84, [128, 2048], F32)

    ps_("psA", [128, 1024])
    ps_("psB", [128, 1024])
    ps_("psC", [128, 512])
    ps_("psD", [128, 512])
    ps_("psE", [128, 512])
    ps_("psT", [128, 1024], BF16)

    sb["psTf"] = sb["psT"][:].bitcast(F32)

    P = _Pl(nc)
    with nc.Block() as block:
        _emit(P, io, sb)
        P.run(block)
    return nc


def _emit(P, io, sb):
    nc = P.nc
    psA, psB, psC, psD, psE, psT = (sb[k] for k in
                                    ("psA", "psB", "psC", "psD", "psE", "psT"))

    RT = {}

    def _init_rt(e):
        pid = e.partition_id()
        RT["g"] = e.to_reg(e.compute_val(pid & 1))
        RT["h128"] = e.to_reg(e.compute_val((pid & 2) * 64))
        RT["g256"] = e.to_reg(e.compute_val((pid & 1) * 256))
        RT["g2048"] = e.to_reg(e.compute_val((pid & 1) * 2048))
        RT["scr"] = e.alloc_register("dynoff")
    P.emit("g", _init_rt)

    def _dyn(e, basekey, const, maxv):
        e.reg_alu(RT["scr"], RT[basekey], const, ALU.add)
        return e.snap(RT["scr"], min_val=0, max_val=maxv)

    def _dynfree(e, off):
        try:
            e.free_register(off.val)
        except Exception:
            pass

    RTS = {}

    def _init_rts(e):
        pid = e.partition_id()
        RTS["g256"] = e.to_reg(e.compute_val((pid & 1) * 256))
        RTS["scr"] = e.alloc_register("dynoffs")
    P.emit("s", _init_rts)

    def _dyn_s(e, const, maxv):
        e.reg_alu(RTS["scr"], RTS["g256"], const, ALU.add)
        return e.snap(RTS["scr"], min_val=0, max_val=maxv)

    pre = []
    pre.append(P.dma("g", sb["starts_i"][0:1, 0:4], io["starts"], stream="pre"))
    pre.append(P.dma("g", sb["bval_sb"][:], io["bval"], stream="pre"))
    pre.append(P.dma("g", sb["arr64_sb"][:], io["arr64"], stream="pre"))
    tnonce = P.dma("g", sb["nonce_sb"][0:1, 0:1], io["nonce"], stream="nonce")
    pre.append(tnonce)
    pre.append(P.dma("g", sb["lao_sb"][:], io["lao"], stream="pre"))
    pre.append(P.dma("g", sb["lbo_sb"][:], io["lbo"], stream="pre"))
    pre.append(P.dma("g", sb["gate_sb"][:], io["gateT"], stream="pre"))

    tms = P.emit("g", lambda e: e.memset(sb["ident"][:], 0.0), inc="ge")
    tid = P.emit("g", lambda e: e.affine_select(
        out=sb["ident"][:], in_=sb["ident"][:], compare_op=ALU.not_equal,
        fill=1.0, base=0, pattern=[[-1, 128]], channel_multiplier=1),
        after=[tms], inc="ge")
    tob = P.emit("v", lambda e: e.memset(sb["ones_bf"][:], 1.0), inc="ve")
    teps = P.emit("v", lambda e: e.memset(sb["epsb"][:], 1e-5), inc="ve")
    tof = P.emit("v", lambda e: e.memset(sb["ones_f1"][:], 1.0), inc="ve")

    t0 = P.emit("v", lambda e: e.tensor_copy(sb["starts_f"][0:1, 0:4],
                                             sb["starts_i"][0:1, 0:4]),
                after=pre, inc="ve")
    t1 = P.emit("v", lambda e: e.tensor_scalar(
        sb["starts_f"][0:1, 4:8], sb["starts_f"][0:1, 0:4],
        sb["bval_sb"][0:1, 0:1], None, ALU.is_lt), after=[t0], inc="ve")
    t2 = P.emit("v", lambda e: e.reduce_sum(
        sb["t11"][0:1, 0:1], sb["starts_f"][0:1, 4:8], axis=AX.X),
        after=[t1], inc="ve")
    t3 = P.emit("v", lambda e: e.tensor_scalar(
        sb["t11"][0:1, 1:2], sb["t11"][0:1, 0:1], -1.0, 0.0, ALU.add, ALU.max),
        after=[t2], inc="ve")
    t4 = P.emit("v", lambda e: e.tensor_scalar(
        sb["t11"][0:1, 2:3], sb["t11"][0:1, 1:2], 3.0, None, ALU.min),
        after=[t3], inc="ve")
    t5 = P.mm(psE[0:64, 0:1], sb["ones_f1"][0:1, 0:64], sb["t11"][0:1, 2:3],
              True, True, after=[t4, tof], inc=True)
    tsel = P.emit("v", lambda e: e.tensor_tensor(
        sb["sel_sb"][:], sb["arr64_sb"][:], psE[0:64, 0:1], ALU.is_equal),
        after=[t5] + pre, inc="ve")
    P.setw("psE", tsel)

    if _STAGES < 1:
        _finish(P, io, sb, [])
        return

    attn_dmas = []
    proj_ps = [(psA, 0), (psA, 512), (psB, 0), (psB, 512)]

    for w in range(2):
        wv = []
        for i in range(4):
            wv.append(P.dma("g", sb["b_wqkv"][:, i * 6144:(i + 1) * 6144],
                            io["wqkv"][w, :, i * 6144:(i + 1) * 6144],
                            after=P.use("swap"), stream="wv"))
        wv.append(P.dma("g", sb["b_rkC"][:], io["ropekC"], after=P.use("swap"), stream="wv"))
        wv.append(P.dma("g", sb["b_rkS"][:], io["ropekS"], after=P.use("swap"), stream="wv"))
        wv.append(P.dma("g", sb["b_rqC"][:], io["ropeqC"], after=P.use("swap"), stream="wv"))
        wv.append(P.dma("g", sb["b_rqS"][:], io["ropeqS"], after=P.use("swap"), stream="wv"))
        wv.append(P.dma("g", sb["b_laqkv"][:], io["laqkv"], after=P.use("swap"), stream="wv"))
        wv.append(P.dma("g", sb["b_lb"][0:64, 0:1024], io["lbq"], after=P.use("swap"), stream="wv"))
        wv.append(P.dma("g", sb["b_lb"][0:64, 1024:2048], io["lbk"], after=P.use("swap"), stream="wv"))
        wv.append(P.dma("g", sb["b_lb"][0:64, 2048:3072], io["lbv"], after=P.use("swap"), stream="wv"))
        wv.append(tsel)
        tvz = P.emit("v", lambda e: e.memset(sb["b_v"][:], 1.0),
                     after=P.use("b_v", "dbgv", "swap"), inc="ve")

        kfm_t, qfm_t, vt_t = [], [], []
        psi = 0
        for pn in range(_NPANEL):
            bi = pn % 2
            xg = sb[f"b_xg{bi}"]
            txg = P.dma("g", xg[:], io["xPH"][pn * 128:(pn + 1) * 128, :],
                        after=P.use(f"b_xg{bi}"), stream=f"xg{bi}")
            if pn % 2 == 0:
                gi = pn // 2

                txq = P.dma("g", sb["b_xq"][:],
                            io["xQH"][gi * 128:(gi + 1) * 128, :],
                            after=P.use("b_xq"), stream="xq")

            lowt = sb[f"b_low{bi}"]
            lw = {}
            specs = []
            if "l" in _SUB:
                specs = [("k", xg, txg, 1, psC, 0), ("v", xg, txg, 2, psD, 256)]
                if pn % 2 == 0:
                    specs.append(("q", sb["b_xq"], txq, 0, psE, 512))
            for name, src, stok, Pi, ps, col in specs:
                pn_tok = None
                for dc in range(NDC):
                    pn_tok = P.mm(
                        ps[0:64, 0:256],
                        sb["b_laqkv"][:, Pi * 1024 + dc * 64: Pi * 1024 + dc * 64 + 64],
                        src[:, dc * 256:(dc + 1) * 256],
                        dc == 0, dc == NDC - 1,
                        after=[stok] + wv + P.use("ps_low_" + name),
                        inc=(dc == NDC - 1))
                ev = P.emit("v", lambda e, d=lowt[0:64, col:col + 256], p=ps:
                            e.tensor_scalar(d, p[0:64, 0:256],
                                            sb["sel_sb"][0:64, 0:1], None, ALU.mult),
                            after=[pn_tok, tsel] + P.use(f"b_low{bi}"), inc="ve")
                P.setw("ps_low_" + name, ev)
                lw[name] = ev

            cons = []
            for oc in (range(4) if "k" in _SUB else []):
                ps, po = proj_ps[psi]
                pw = P.use(f"proj{psi}")
                for dc in range(NDC):
                    P.mm(ps[:, po:po + 256],
                         sb["b_wqkv"][:, 8192 + dc * 512 + oc * 128:
                                      8192 + dc * 512 + oc * 128 + 128],
                         xg[:, dc * 256:(dc + 1) * 256],
                         dc == 0, False, after=[txg] + wv + pw)
                tdel = P.mm(ps[:, po:po + 256],
                            sb["b_lb"][:, 1024 + w * 512 + oc * 128:
                                       1024 + w * 512 + oc * 128 + 128],
                            lowt[0:64, 0:256], False, True,
                            after=[lw["k"]], inc=True)
                if "r" not in _SUB:
                    co = oc * S + pn * 256
                    tk1 = P.emit("a", lambda e, p=ps, po=po, co=co: e.copy(
                        sb["b_kfm"][:, co:co + 256], p[:, po:po + 256]),
                        after=[tdel] + P.use(f"proj{psi}_e"), inc="ae")
                    kfm_t += [tk1]
                    P.setw(f"proj{psi}", tk1)
                    cons.append(tk1)
                    psi = (psi + 1) % 4
                    continue
                if "Z" in _SUB:
                    tuu = tdel
                else:
                    tuu = P.emit("a", lambda e, p=ps, po=po: e.copy(
                        sb["b_uu"][:, 0:256], p[:, po:po + 256]),
                        after=[tdel] + P.use("b_uu"), inc="ae")
                if "Y" in _SUB:
                    tc2 = tuu
                elif "X" in _SUB:
                    tc2 = P.emit("a", lambda e: e.copy(
                        sb["b_us"][:, 0:256], sb["b_uu"][:, 0:256]),
                        after=[tuu] + P.use("b_us"), inc="ae")
                else:
                    P.emit("a", lambda e: e.copy(
                        sb["b_us"][0:64, 0:256], sb["b_uu"][64:128, 0:256]),
                        after=[tuu] + P.use("b_us"))
                    tc2 = P.emit("a", lambda e: e.copy(
                        sb["b_us"][64:128, 0:256], sb["b_uu"][0:64, 0:256]),
                        inc="ae")
                tc1 = tc2
                tm1 = P.emit("v", lambda e, pn=pn: e.tensor_tensor(
                    sb["b_m1"][:, 0:256], sb["b_uu"][:, 0:256],
                    sb["b_rkC"][:, pn * 256:(pn + 1) * 256], ALU.mult),
                    after=[tuu] + P.use("b_m1"), inc="ve")
                if "1" in _SUB:
                    co = oc * S + pn * 256
                    tk1 = P.emit("v", lambda e, co=co: e.tensor_copy(
                        sb["b_kfm"][:, co:co + 256], sb["b_m1"][:, 0:256]),
                        after=[tm1, tc1, tc2], inc="ve")
                    tm2 = tk1
                else:
                    tm2 = P.emit("g" if "P" in _SUB else "v",
                                 lambda e, pn=pn: e.tensor_tensor(
                        sb["b_m2"][:, 0:256], sb["b_us"][:, 0:256],
                        sb["b_rkS"][:, pn * 256:(pn + 1) * 256], ALU.mult),
                        after=[tc1, tc2] + P.use("b_m2"),
                        inc=("ge" if "P" in _SUB else "ve"))
                    co = oc * S + pn * 256
                    tk1 = P.emit("v", lambda e, co=co: e.tensor_tensor(
                        sb["b_kfm"][:, co:co + 256], sb["b_m1"][:, 0:256],
                        sb["b_m2"][:, 0:256], ALU.add),
                        after=[tm1, tm2], inc="ve")
                kfm_t += [tk1]
                P.setw(f"proj{psi}", [tuu])
                P.setw("b_uu", [tc2, tm1])
                P.setw("b_us", tm2)
                P.setw("b_m1", tk1)
                P.setw("b_m2", tk1)
                cons.append(tm1)
                cons.append(tuu)
                psi = (psi + 1) % 4

            for tcc in (range(2) if "v" in _SUB else []):
                ps, po = proj_ps[psi]
                pw = P.use(f"proj{psi}")
                for dc in range(NDC):
                    P.mm(ps[:, po:po + 512],
                         xg[:, dc * 256 + tcc * 128: dc * 256 + tcc * 128 + 128],
                         sb["b_wqkv"][:, 16384 + dc * 512: 16384 + dc * 512 + 512],
                         dc == 0, False, after=[txg] + wv + pw)
                tdel = P.mm(ps[:, po:po + 512],
                            lowt[0:64, 256 + tcc * 128: 256 + tcc * 128 + 128],
                            sb["b_lb"][:, 2048 + w * 512: 2048 + w * 512 + 512],
                            False, True, after=[lw["v"]], inc=True)
                kc = pn * 2 + tcc
                tv = P.emit("a", lambda e, p=ps, po=po, kc=kc: e.copy(
                    sb["b_v"][:].rearrange("p (k h c) -> p k h c", k=32, h=4)[
                        :, kc, :, 0:128],
                    p[:, po:po + 512].rearrange("p (h c) -> p h c", h=4)),
                    after=[tdel, tvz], inc="ae")
                vt_t.append(tv)
                P.setw(f"proj{psi}", tv)
                cons.append(tv)
                psi = (psi + 1) % 4

            if pn % 2 == 0 and "q" in _SUB:
                gi = pn // 2
                for oc in range(4):
                    ps, po = proj_ps[psi]
                    pw = P.use(f"proj{psi}")
                    for dc in range(NDC):
                        P.mm(ps[:, po:po + 256],
                             sb["b_wqkv"][:, dc * 512 + oc * 128:
                                          dc * 512 + oc * 128 + 128],
                             sb["b_xq"][:, dc * 256:(dc + 1) * 256],
                             dc == 0, False, after=[txq] + wv + pw)
                    tdel = P.mm(ps[:, po:po + 256],
                                sb["b_lb"][:, w * 512 + oc * 128:
                                           w * 512 + oc * 128 + 128],
                                lowt[0:64, 512:768], False, True,
                                after=[lw["q"]], inc=True)
                    tuu = P.emit("a", lambda e, p=ps, po=po: e.copy(
                        sb["b_uu"][:, 256:512], p[:, po:po + 256]),
                        after=[tdel] + P.use("b_uuq"), inc="ae")
                    P.emit("a", lambda e: e.copy(
                        sb["b_us"][0:64, 256:512], sb["b_uu"][64:128, 256:512]),
                        after=[tuu] + P.use("b_usq"))
                    tc2 = P.emit("a", lambda e: e.copy(
                        sb["b_us"][64:128, 256:512], sb["b_uu"][0:64, 256:512]),
                        inc="ae")
                    tc1 = tc2
                    tm1 = P.emit("v", lambda e, gi=gi: e.tensor_tensor(
                        sb["b_m1"][:, 256:512], sb["b_uu"][:, 256:512],
                        sb["b_rqC"][:, gi * 256:(gi + 1) * 256], ALU.mult),
                        after=[tuu] + P.use("b_m1q"), inc="ve")
                    tm2 = P.emit("g", lambda e, gi=gi: e.tensor_tensor(
                        sb["b_m2"][:, 256:512], sb["b_us"][:, 256:512],
                        sb["b_rqS"][:, gi * 256:(gi + 1) * 256], ALU.mult),
                        after=[tc1, tc2] + P.use("b_m2q"), inc="ge")
                    co = oc * 2048 + gi * 256
                    tq1 = P.emit("v", lambda e, co=co: e.tensor_tensor(
                        sb["b_qfm"][:, co:co + 256], sb["b_m1"][:, 256:512],
                        sb["b_m2"][:, 256:512], ALU.add),
                        after=[tm1, tm2] + P.use("dbgqk"), inc="ve")
                    qfm_t += [tq1]
                    P.setw(f"proj{psi}", [tuu])
                    P.setw("b_uuq", [tc2, tm1])
                    P.setw("b_usq", tm2)
                    P.setw("b_m1q", tq1)
                    P.setw("b_m2q", tq1)
                    cons.append(tm1)
                    cons.append(tuu)
                    psi = (psi + 1) % 4
                P.setw("b_xq", _lastl(cons))
            P.setw(f"b_xg{bi}", _lastl(cons))
            P.setw(f"b_low{bi}", _lastl(cons))

        proj_done = kfm_t + qfm_t + vt_t
        if _DEBUG and w == 0:
            td1 = P.dma("s", io["dbgq"], sb["b_qfm"][:], after=proj_done, stream="dbg")
            td2 = P.dma("s", io["dbgk"], sb["b_kfm"][:], after=proj_done, stream="dbg")
            td3 = P.dma("s", io["dbgv"], sb["b_v"][:], after=proj_done, stream="dbg")
            P.setw("dbgv", [td3])
            P.setw("dbgqk", [td1, td2])
        if _STAGES < 2:
            _finish(P, io, sb, [])
            return

        tmask = P.dma("s", sb["b_masks"][:], io["masks"], after=proj_done, stream="mask")

        wave_ev = []
        stg = sb["b_astg"]
        for hl in range(4):
            qcol, kcol = hl * 2048, hl * S
            tq2m = P.emit("v", lambda e, qcol=qcol: e.tensor_tensor(
                sb["b_qsq"][:, 0:2048], sb["b_qfm"][:, qcol:qcol + 2048],
                sb["b_qfm"][:, qcol:qcol + 2048], ALU.mult),
                after=proj_done + P.use("b_qsq"), inc="ve")
            nt = []
            prev = tq2m
            for blk, nblk in ((0, 4), (1, 8)):
                if blk == 1:
                    prev = P.emit("v", lambda e, kcol=kcol: e.tensor_tensor(
                        sb["b_qsq"][:, 0:4096], sb["b_kfm"][:, kcol:kcol + S],
                        sb["b_kfm"][:, kcol:kcol + S], ALU.mult),
                        after=nt + [prev], inc="ve")
                for r in range(nblk):
                    tm1 = P.mm(psC[0:1, 0:512], sb["ones_bf"][:],
                               sb["b_qsq"][:, r * 512:(r + 1) * 512], True, True,
                               after=[prev, tob] + P.use("psC", "ps_av0"),
                               inc=True)
                    tr = P.emit("v", lambda e: e.reduce_max(
                        sb["t11"][0:1, 8:9], psC[0:1, 0:512], axis=AX.X),
                        after=[tm1], inc="ve")
                    P.setw("psC", tr)
                    if r == 0:
                        ta = P.emit("v", lambda e, o=blk: e.tensor_copy(
                            sb["t11"][0:1, 4 + o:5 + o], sb["t11"][0:1, 8:9]),
                            after=[tr], inc="ve")
                    else:
                        ta = P.emit("v", lambda e, o=blk: e.tensor_tensor(
                            sb["t11"][0:1, 4 + o:5 + o], sb["t11"][0:1, 4 + o:5 + o],
                            sb["t11"][0:1, 8:9], ALU.max), after=[tr], inc="ve")
                    prev = ta
                    nt.append(ta)
                    P.setw("b_qsq", tm1)
            tpr = P.emit("v", lambda e: e.tensor_tensor(
                sb["t11"][0:1, 6:7], sb["t11"][0:1, 4:5], sb["t11"][0:1, 5:6],
                ALU.mult), after=nt, inc="ve")
            tsq = P.emit("a", lambda e: e.activation(
                sb["t11"][0:1, 7:8], sb["t11"][0:1, 6:7], AF.Sqrt),
                after=[tpr], inc="ae")
            tng = P.emit("v", lambda e: e.tensor_scalar(
                sb["t11"][0:1, 9:10], sb["t11"][0:1, 7:8], -1.0, None, ALU.mult),
                after=[tsq], inc="ve")
            tbc = P.mm(psD[0:128, 0:1], sb["ones_f1"][0:1, 0:128],
                       sb["t11"][0:1, 9:10], True, True,
                       after=[tng, tof] + P.use("psD", "ps_av1"),
                       inc=True)
            tbias = P.emit("a", lambda e: e.copy(sb["bias_h"][:], psD[0:128, 0:1]),
                           after=[tbc] + P.use("bias_h"), inc="ae")
            P.setw("psD", tbias)

            for qg in range(4):
                av_done = []
                for kcp in range(4 * qg + 4):
                    pt = sb[f"b_pt{kcp % 4}"]
                    ps = (psA, psB)[kcp % 2]
                    mm_t = []
                    for kk in range(2):
                        kc = kcp * 2 + kk
                        tmm = P.mm(ps[:, kk * 512:kk * 512 + 512],
                                   sb["b_kfm"][:, kcol + kc * 128:
                                               kcol + kc * 128 + 128],
                                   sb["b_qfm"][:, qcol + qg * 512:
                                               qcol + qg * 512 + 512],
                                   True, True,
                                   after=[tmask, tbias] + proj_done +
                                   P.use(f"ps_sc{kcp % 2}"), inc=True)
                        mm_t.append(tmm)
                    mk_t = [[], []]
                    for kk in range(2):
                        kc = kcp * 2 + kk
                        for qq in range(4):
                            qc = qg * 4 + qq
                            d0 = 4 * (qc // 2) + (qc % 2)
                            sl = (ps, kk * 512 + qq * 128)
                            if d0 <= kc <= d0 + 2:
                                j = kc - d0
                                mk_t[kk].append(P.emit(
                                    "v", lambda e, s=sl, qc=qc, j=j: e.tensor_tensor(
                                        s[0][:, s[1]:s[1] + 128],
                                        s[0][:, s[1]:s[1] + 128],
                                        sb["b_masks"][:, (qc * 3 + j) * 128:
                                                      (qc * 3 + j) * 128 + 128],
                                        ALU.add),
                                    after=[mm_t[kk], tmask], inc="ve"))
                            elif kc > d0 + 2:
                                mk_t[kk].append(P.emit(
                                    "v", lambda e, s=sl: e.memset(
                                        s[0][:, s[1]:s[1] + 128], -1e9),
                                    after=[mm_t[kk]], inc="ve"))
                    texps = []
                    for kk in range(2):
                        texps.append(P.emit(
                            "a", lambda e, pt=pt, ps=ps, kk=kk: e.activation(
                                pt[:, kk * 512:kk * 512 + 512],
                                ps[:, kk * 512:kk * 512 + 512],
                                AF.Exp, bias=sb["bias_h"][0:128, 0:1]),
                            after=[mm_t[kk]] + mk_t[kk] + [tbias] +
                            P.use(f"b_pt{kcp % 4}"), inc="ae"))
                    texp = texps[1]
                    P.setw(f"ps_sc{kcp % 2}", [texps[0], texps[1]])
                    P.setw("bias_h", texp)
                    last_av = None
                    for kk in range(2):
                        kc = kcp * 2 + kk
                        for qq in range(4):
                            qc = qg * 4 + qq
                            d0 = 4 * (qc // 2) + (qc % 2)
                            if kc > d0 + 2:
                                continue
                            avp = (psC, psD, psE, sb["psTf"])[qq]
                            avs = avp[:, 0:129] if qq < 3 else avp[:, 0:129]
                            tav = P.mm(avs,
                                       pt[:, kk * 512 + qq * 128:
                                          kk * 512 + qq * 128 + 128],
                                       sb["b_v"][:].rearrange(
                                           "p (k x) -> p k x", k=32)[
                                           :, kc, hl * 129:hl * 129 + 129],
                                       kc == 0, kc == d0 + 2,
                                       after=[texps[kk], tvz] + vt_t +
                                       P.use(f"ps_av{qq}", "psC", "psD", "psE",
                                             "psT"),
                                       inc=(kc == d0 + 2))
                            if tav:
                                av_done.append(tav)
                                last_av = tav
                    P.setw(f"b_pt{kcp % 4}", last_av)
                rc_t = []
                for qq in range(4):
                    avp = (psC, psD, psE, sb["psTf"])[qq]
                    rc_t.append(P.emit("v", lambda e, avp=avp, qq=qq: e.reciprocal(
                        sb["rstd"][:, qq:qq + 1], avp[:, 128:129]),
                        after=av_done + P.use(f"rstd{qq}"), inc="ve"))
                ev_t = []
                for qq in range(4):
                    avp = (psC, psD, psE, sb["psTf"])[qq]
                    co = (qg * 4 + qq) * 512 + hl * 128
                    tev = P.emit("v", lambda e, avp=avp, qq=qq, co=co:
                                 e.tensor_scalar(
                                     stg[:, co:co + 128],
                                     avp[:, 0:128],
                                     sb["rstd"][:, qq:qq + 1], None, ALU.mult),
                                 after=rc_t + P.use("b_astg"), inc="ve")
                    ev_t.append(tev)
                    P.setw(f"ps_av{qq}", tev)
                    P.setw(f"rstd{qq}", tev)
                wave_ev += ev_t

        def attn_dma(e, w=w):
            off = _dyn(e, "g2048", 0, 2048)
            dst = io["shpf"][ds(off, 2048), w * 512:w * 512 + 512]
            inst = e.dma_start(
                out=dst.rearrange("(a p) c -> p a c", p=128),
                in_=stg[:].rearrange("p (a c) -> p a c", a=16))
            _dynfree(e, off)
            return inst
        ta = P.emit("g", attn_dma, after=wave_ev, inc="attnw", dma=True)
        attn_dmas.append(ta)
        P.setw("b_astg", ta)
        P.setw("swap", _lastl(attn_dmas))
        P.setw("b_v", _lastl(attn_dmas))
        P.setw("b_qsq", _lastl(attn_dmas))

    if _STAGES < 3:
        _finish(P, io, sb, [])
        return

    wo_ld = []
    for i in range(8):
        wo_ld.append(P.dma("g", sb["d_wo"][:, i * 4096:(i + 1) * 4096],
                           io["wo"][:, i * 4096:(i + 1) * 4096],
                           after=attn_dmas, stream="wo"))


    def _flagw(e):
        off = _dyn(e, "g", 0, 1)
        inst = e.dma_start(out=io["shf"][ds(off, 1), 0:1],
                           in_=sb["nonce_sb"][0:1, 0:1])
        _dynfree(e, off)
        return inst
    tflag = P.emit("g", _flagw, after=attn_dmas + [tnonce], inc="flag", dma=True)

    ld_base = 0
    ld_h = P._getsem("pol")[0]

    def poll(e, ld_base=ld_base, ld_h=ld_h):
        nv = e.value_load(sb["nonce_sb"][0:1, 0:1])
        other = e.compute_val((e.partition_id() & 1) != 1)
        cr = e.alloc_register("pollc")
        e.reg_mov(cr, ld_base)

        def cond():
            e.dma_start(out=sb["flag_sb"][0:1, 0:1],
                        in_=io["shf"][ds(other, 1), 0:1]).then_inc(ld_h, 16)
            e.reg_add(cr, cr, 16)
            e.wait_ge(ld_h, e.snap(cr))
            f = e.value_load(sb["flag_sb"][0:1, 0:1])
            return f != nv
        with e.While(cond):
            pass
    P.emit("g", poll, after=[tflag])
    tpoll = P.emit("g", lambda e: e.nop(), inc="ge")

    res_t = []
    shpb = io["shpf"].rearrange("(b r) c -> b r c", r=512)
    for g4 in range(2):
        attP = sb["d_attnP"]

        def rdj(e, g4=g4, slot=0, jj=0):
            bb = _dyn_s(e, 0, 256)
            srcv = shpb[slot * 4 + 2 * g4 + jj,
                        ds(bb, 256), :].rearrange("(s p) c -> p s c", p=128)
            dstv = attP[:].rearrange("p (t c) -> p t c", t=4)[
                :, 2 * jj:2 * jj + 2, slot * 1024:slot * 1024 + 1024]
            inst = e.dma_start(out=dstv, in_=srcv)
            _dynfree(e, bb)
            return inst
        rd_t = []
        for slot in range(2):
            for jj in range(2):
                rd_t.append(P.emit(
                    "s", lambda e, f=rdj, slot=slot, jj=jj: f(e, slot=slot, jj=jj),
                    after=[tpoll] + P.use("d_attnP"), inc="rd", dma=True))
        ta0 = ta1 = rd_t[-1]

        grp_cons = []
        for sub in range(4):
            tc = 4 * g4 + sub
            att = attP[:, sub * 2048:(sub + 1) * 2048]
            txr = P.dma("a", sb["d_xres"][:], io["xres"][tc],
                        after=P.use("d_xres"), stream="xres")
            afm = sb["d_afm"]
            tr_t = []
            for fq in range(4):
                tt = None
                for f2 in range(4):
                    fc = fq * 4 + f2
                    tt = P.emit("t", lambda e, fc=fc, f2=f2, att=att: e.transpose(
                        psT[:, f2 * 128:f2 * 128 + 128],
                        att[:, fc * 128:fc * 128 + 128], sb["ident"][:]),
                        after=[ta0, ta1, tid] + P.use("psT", "d_afm"),
                        inc=("te" if f2 == 3 else None))
                tev = P.emit("a", lambda e, fq=fq, afm=afm: e.copy(
                    afm[:, fq * 512:fq * 512 + 512], psT[:, 0:512]),
                    after=[tt] + P.use("d_afm"), inc="ae")
                P.setw("psT", tev)
                tr_t.append(tev)
            grp_cons.append(tr_t[-1])

            tlo_mm = None
            for fc in range(NDC):
                tlo_mm = P.mm(psE[0:64, 0:128],
                              sb["lao_sb"][:, fc * 64:fc * 64 + 64],
                              afm[:, fc * 128:fc * 128 + 128],
                              fc == 0, fc == NDC - 1,
                              after=tr_t + P.use("psE"), inc=(fc == NDC - 1))
            tlo = P.emit("v", lambda e, tc=tc: e.tensor_scalar(
                sb["d_lowo"][0:64, tc * 128:tc * 128 + 128], psE[0:64, 0:128],
                sb["sel_sb"][0:64, 0:1], None, ALU.mult),
                after=[tlo_mm, tsel], inc="ve")
            P.setw("psE", tlo)

            for fc in range(NDC):
                for og in range(4):
                    ps = (psA, psB)[og // 2]
                    P.mm(ps[:, (og % 2) * 512:(og % 2) * 512 + 512],
                         afm[:, fc * 128:fc * 128 + 128],
                         sb["d_wo"][:, fc * 2048 + og * 512:
                                    fc * 2048 + og * 512 + 512],
                         fc == 0, False,
                         after=wo_ld + tr_t + P.use(f"ps_wo{og}"))
            del_t = []
            for og in range(4):
                ps = (psA, psB)[og // 2]
                del_t.append(P.mm(ps[:, (og % 2) * 512:(og % 2) * 512 + 512],
                                  sb["d_lowo"][0:64, tc * 128:tc * 128 + 128],
                                  sb["lbo_sb"][:, og * 512:og * 512 + 512],
                                  False, True, after=[tlo], inc=True))
            ev_last = None
            for og in range(4):
                ps = (psA, psB)[og // 2]
                tev = P.emit("v", lambda e, ps=ps, og=og, tc=tc: e.tensor_tensor(
                    sb["res"][:, tc * 2048 + og * 512: tc * 2048 + og * 512 + 512],
                    ps[:, (og % 2) * 512:(og % 2) * 512 + 512],
                    sb["d_xres"][:, og * 512:og * 512 + 512], ALU.add),
                    after=del_t + [txr], inc="ve")
                P.setw(f"ps_wo{og}", tev)
                res_t.append(tev)
                ev_last = tev
            P.setw("d_afm", ev_last)
            P.setw("d_xres", ev_last)
            if _DEBUG:
                P.dma("s", io["dbgattn"][tc], att, after=tr_t, stream="dbg")
        P.setw("d_attnP", _lastl(grp_cons))

    x1_t = []
    for tc in range(8):
        rs = sb["res"][:, tc * 2048:(tc + 1) * 2048]
        tm = P.emit("v", lambda e, rs=rs: e.reduce_sum(
            sb["tiny"][:, 0:1], rs, axis=AX.X), after=res_t + P.use("ln1"),
            inc="ve")
        tm2 = P.emit("v", lambda e: e.tensor_scalar(
            sb["tiny"][:, 1:2], sb["tiny"][:, 0:1], 1.0 / 2048.0, None, ALU.mult),
            after=[tm], inc="ve")
        tsb = P.emit("v", lambda e, rs=rs: e.tensor_scalar(
            rs, rs, sb["tiny"][:, 1:2], None, ALU.subtract),
            after=[tm2], inc="ve")
        tq = P.emit("a", lambda e, rs=rs: e.activation(
            sb["d_sq"][:], rs, AF.Square, accum_out=sb["ssum"][:, 0:1]),
            after=[tsb] + P.use("d_sq"), inc="ae")
        P.setw("d_sq", tq)
        tsd = P.emit("a", lambda e: e.activation(
            sb["tiny"][:, 2:3], sb["ssum"][:, 0:1], AF.Sqrt,
            bias=sb["epsb"][:, 0:1], scale=1.0 / 2048.0),
            after=[tq, teps], inc="ae")
        trc = P.emit("v", lambda e: e.reciprocal(
            sb["tiny"][:, 3:4], sb["tiny"][:, 2:3]), after=[tsd], inc="ve")
        tml = P.emit("v", lambda e, rs=rs: e.tensor_scalar(
            rs, rs, sb["tiny"][:, 3:4], None, ALU.mult), after=[trc], inc="ve")
        tct = P.emit("a", lambda e, rs=rs: e.copy(sb["d_x1tm"][:], rs),
                     after=[tml] + P.use("d_x1tm"), inc="ae")
        P.setw("ln1", tct)
        for fq in range(4):
            tt = None
            for f2 in range(4):
                dc = fq * 4 + f2
                tt = P.emit("t", lambda e, dc=dc, f2=f2: e.transpose(
                    psT[:, f2 * 128:f2 * 128 + 128],
                    sb["d_x1tm"][:, dc * 128:dc * 128 + 128], sb["ident"][:]),
                    after=[tct, tid] + P.use("psT"),
                    inc=("te" if f2 == 3 else None))
            tev = P.emit("a", lambda e, fq=fq, tc=tc: e.copy(
                sb["x1fm"][:].rearrange("p (dc t) -> p dc t", dc=NDC)[
                    :, fq * 4:fq * 4 + 4, tc * 128:tc * 128 + 128],
                psT[:, 0:512].rearrange("p (a c) -> p a c", a=4)),
                after=[tt], inc="ae")
            P.setw("psT", tev)
            P.setw("d_x1tm", tev)
            x1_t.append(tev)
        if _DEBUG:
            P.dma("s", io["dbgx1"][tc], rs, after=[tml], stream="dbg")

    if _STAGES < 4:
        _finish(P, io, sb, x1_t)
        return

    g_t = []
    for tc in range(8):
        tgm = None
        for dc in range(NDC):
            tgm = P.mm(psE[:, 0:16],
                       sb["x1fm"][:, dc * 1024 + tc * 128: dc * 1024 + tc * 128 + 128],
                       sb["gate_sb"][:, dc * 16:(dc + 1) * 16],
                       dc == 0, dc == NDC - 1,
                       after=x1_t + pre + P.use("psE"), inc=(dc == NDC - 1))
        tmx = P.emit("v", lambda e: e.reduce_max(
            sb["tiny"][:, 4:5], psE[:, 0:16], axis=AX.X), after=[tgm], inc="ve")
        tng = P.emit("v", lambda e: e.tensor_scalar(
            sb["tiny"][:, 5:6], sb["tiny"][:, 4:5], -1.0, None, ALU.mult),
            after=[tmx], inc="ve")
        wg = sb["wg_sb"]
        tex = P.emit("a", lambda e, tc=tc: e.activation(
            wg[:, tc * 16:tc * 16 + 16], psE[:, 0:16], AF.Exp,
            bias=sb["tiny"][:, 5:6], accum_out=sb["ssum"][:, 1:2]),
            after=[tng], inc="ae")
        P.setw("psE", tex)
        trd = P.emit("v", lambda e: e.reciprocal(
            sb["tiny"][:, 7:8], sb["ssum"][:, 1:2]), after=[tex], inc="ve")
        tm1 = P.emit("v", lambda e, tc=tc: e.reduce_max(
            sb["tiny"][:, 8:9], wg[:, tc * 16:tc * 16 + 16], axis=AX.X),
            after=[tex], inc="ve")
        teq = P.emit("v", lambda e, tc=tc: e.tensor_scalar(
            sb["gsc"][:, 0:16], wg[:, tc * 16:tc * 16 + 16],
            sb["tiny"][:, 8:9], None, ALU.is_equal), after=[tm1], inc="ve")
        tp2 = P.emit("v", lambda e, tc=tc: e.scalar_tensor_tensor(
            sb["gsc"][:, 16:32], sb["gsc"][:, 0:16], -2.0,
            wg[:, tc * 16:tc * 16 + 16], ALU.mult, ALU.add),
            after=[teq], inc="ve")
        tm2x = P.emit("v", lambda e: e.reduce_max(
            sb["tiny"][:, 9:10], sb["gsc"][:, 16:32], axis=AX.X),
            after=[tp2], inc="ve")
        tge = P.emit("v", lambda e, tc=tc: e.tensor_scalar(
            sb["gsc"][:, 32:48], wg[:, tc * 16:tc * 16 + 16],
            sb["tiny"][:, 9:10], None, ALU.is_ge), after=[tm2x], inc="ve")
        tw1 = P.emit("v", lambda e, tc=tc: e.tensor_tensor(
            wg[:, tc * 16:tc * 16 + 16], wg[:, tc * 16:tc * 16 + 16],
            sb["gsc"][:, 32:48], ALU.mult), after=[tge], inc="ve")
        tw2 = P.emit("v", lambda e, tc=tc: e.tensor_scalar(
            wg[:, tc * 16:tc * 16 + 16], wg[:, tc * 16:tc * 16 + 16],
            sb["tiny"][:, 7:8], None, ALU.mult), after=[tw1, trd], inc="ve")
        g_t.append(tw2)
    if _DEBUG:
        P.dma("s", io["dbggw"], sb["wg_sb"][:], after=g_t, stream="dbg")

    y_last = []
    for ei in range(NE + 2):
        shared = ei >= NE
        if shared:
            s1, s3, s2 = (io["sw1T"][ei - NE], io["sw3T"][ei - NE],
                          io["sw2T"][ei - NE])
        else:
            s1, s3, s2 = io["w1T"][ei], io["w3T"][ei], io["w2T"][ei]

        h_t = []
        for ic in range(ICN):
            sl = ic % 2
            ps1h = (psC, psE)[ic % 2]
            ps3h = (psD, sb["psTf"])[ic % 2]
            k1 = ("psC", "psE")[ic % 2]
            k3 = ("psD", "psT")[ic % 2]
            t1l = P.dma("g", sb[f"e_w1r{sl}"][:], s1[ic],
                        after=x1_t + P.use(f"e_w1r{sl}"), stream=f"w1r{sl}")
            t3l = P.dma("g", sb[f"e_w3r{sl}"][:], s3[ic],
                        after=x1_t + P.use(f"e_w3r{sl}"), stream=f"w3r{sl}")
            for tg in range(2):
                th1 = None
                for dc in range(NDC):
                    th1 = P.mm(ps1h[:, 0:512],
                               sb[f"e_w1r{sl}"][:, dc * 128:dc * 128 + 128],
                               sb["x1fm"][:, dc * 1024 + tg * 512:
                                          dc * 1024 + tg * 512 + 512],
                               dc == 0, dc == NDC - 1,
                               after=[t1l] + x1_t + P.use(k1),
                               inc=(dc == NDC - 1))
                th3 = None
                for dc in range(NDC):
                    th3 = P.mm(ps3h[:, 0:512],
                               sb[f"e_w3r{sl}"][:, dc * 128:dc * 128 + 128],
                               sb["x1fm"][:, dc * 1024 + tg * 512:
                                          dc * 1024 + tg * 512 + 512],
                               dc == 0, dc == NDC - 1,
                               after=[t3l] + x1_t + P.use(k3),
                               inc=(dc == NDC - 1))
                t1s = (ic % 2) * 512
                tsl_ = P.emit("a", lambda e, ps1h=ps1h, t1s=t1s: e.activation(
                    sb["e_t1"][:, t1s:t1s + 512],
                    ps1h[:, 0:512], AF.Silu),
                    after=[th1] + P.use(f"e_t1{ic % 2}"), inc="ae")
                P.setw(k1, tsl_)
                thm = P.emit("v", lambda e, ic=ic, tg=tg, ps3h=ps3h, t1s=t1s:
                             e.tensor_tensor(
                    sb["e_h"][:, ic * 1024 + tg * 512: ic * 1024 + tg * 512 + 512],
                    sb["e_t1"][:, t1s:t1s + 512], ps3h[:, 0:512], ALU.mult),
                    after=[tsl_, th3] + P.use("e_h"), inc="ve")
                P.setw(k3, thm)
                P.setw(f"e_t1{ic % 2}", thm)
                h_t.append(thm)
            P.setw(f"e_w1r{sl}", h_t[-1])
            P.setw(f"e_w3r{sl}", h_t[-1])
        w2_ld = []
        for ic in range(ICN):
            w2_ld.append(P.dma("g", sb["e_w2"][:, ic * 2048:(ic + 1) * 2048],
                               s2[ic], after=x1_t + P.use("e_w2"), stream="w2"))
        if _DEBUG and ei == 0:
            P.dma("s", io["dbgh"], sb["e_h"][:], after=h_t, stream="dbg")
        y_t = []
        for tg in range(2):
            for tcc in range(4):
                tci = tg * 4 + tcc
                ty4 = [None] * 4
                for ic in range(ICN):
                    for og in range(4):
                        ps = (psA, psB)[og // 2]
                        psl = ps[:, (og % 2) * 512:(og % 2) * 512 + 512]
                        ty4[og] = P.mm(
                            psl,
                            sb["e_h"][:, ic * 1024 + tg * 512 + tcc * 128:
                                      ic * 1024 + tg * 512 + tcc * 128 + 128],
                            sb["e_w2"][:, ic * 2048 + og * 512:
                                       ic * 2048 + og * 512 + 512],
                            ic == 0, ic == ICN - 1,
                            after=h_t + w2_ld +
                            P.use("ps_y0", "ps_y1", "ps_y2", "ps_y3"),
                            inc=(ic == ICN - 1))
                for og in range(4):
                    ps = (psA, psB)[og // 2]
                    psl = ps[:, (og % 2) * 512:(og % 2) * 512 + 512]
                    rsl = sb["res"][:, tci * 2048 + og * 512:
                                    tci * 2048 + og * 512 + 512]
                    if shared:
                        tev = P.emit("v", lambda e, p=psl, r=rsl:
                                     e.scalar_tensor_tensor(
                                         r, p, 1.0, r, ALU.mult, ALU.add),
                                     after=[ty4[og]] + g_t + P.use(f"res{tci}_{og}"),
                                     inc="ve")
                    else:
                        tev = P.emit("v", lambda e, p=psl, r=rsl, tci=tci, ei=ei:
                                     e.scalar_tensor_tensor(
                                         r, p,
                                         sb["wg_sb"][:, tci * 16 + ei:
                                                     tci * 16 + ei + 1],
                                         r, ALU.mult, ALU.add),
                                     after=[ty4[og]] + g_t + P.use(f"res{tci}_{og}"),
                                     inc="ve")
                    P.setw(f"ps_y{og}", tev)
                    P.setw(f"res{tci}_{og}", tev)
                    y_t.append(tev)
        P.setw("e_h", y_t[-1])
        P.setw("e_w2", y_t[-1])
        y_last = y_t

    all_y = [P.war[f"res{t}_{o}"] for t in range(8) for o in range(4)]

    out_t = []
    for tc in range(8):
        rs = sb["res"][:, tc * 2048:(tc + 1) * 2048]
        tm = P.emit("v", lambda e, rs=rs: e.reduce_sum(
            sb["tiny"][:, 10:11], rs, axis=AX.X), after=all_y, inc="ve")
        tm2 = P.emit("v", lambda e: e.tensor_scalar(
            sb["tiny"][:, 11:12], sb["tiny"][:, 10:11], 1.0 / 2048.0, None,
            ALU.mult), after=[tm], inc="ve")
        tsb = P.emit("v", lambda e, rs=rs: e.tensor_scalar(
            rs, rs, sb["tiny"][:, 11:12], None, ALU.subtract),
            after=[tm2], inc="ve")
        tq = P.emit("a", lambda e, rs=rs: e.activation(
            sb["d_sq"][:], rs, AF.Square, accum_out=sb["ssum"][:, 2:3]),
            after=[tsb] + P.use("d_sq"), inc="ae")
        P.setw("d_sq", tq)
        tsd = P.emit("a", lambda e: e.activation(
            sb["tiny"][:, 12:13], sb["ssum"][:, 2:3], AF.Sqrt,
            bias=sb["epsb"][:, 0:1], scale=1.0 / 2048.0),
            after=[tq, teps], inc="ae")
        trc = P.emit("v", lambda e: e.reciprocal(
            sb["tiny"][:, 13:14], sb["tiny"][:, 12:13]), after=[tsd], inc="ve")
        tml = P.emit("v", lambda e, rs=rs: e.tensor_scalar(
            sb["e_oscr"][:], rs, sb["tiny"][:, 13:14], None, ALU.mult),
            after=[trc] + P.use("e_oscr"), inc="ve")
        tst = P.dma("s", io["out"][tc], sb["e_oscr"][:], after=[tml], stream="out")
        P.setw("e_oscr", tst)
        out_t.append(tst)

    _finish(P, io, sb, out_t)


def _finish(P, io, sb, out_t):
    if not out_t:
        allj = [(sn, rec[1]) for sn, rec in P.sem.items() if rec[1] > 0]
        tz = P.emit("v", lambda e: e.memset(sb["e_oscr"][:], 0.0),
                    after=allj, inc="ve")
        out_t = [P.dma("s", io["out"][tc], sb["e_oscr"][:], after=[tz],
                       stream="out") for tc in range(8)]
    P.emit("s", lambda e: e.nop(), after=out_t)


def _last(*toks):
    toks = [t for t in toks if t is not None]
    return toks[-1] if toks else None


def _lastl(toks):
    toks = [t for t in toks if t is not None]
    if not toks:
        return None
    best = {}
    for sn, v in toks:
        best[sn] = max(best.get(sn, 0), v)
    sn, v = max(best.items(), key=lambda kv: kv[1])
    return (sn, v)


_CACHE = {}


def kernel(**inputs):
    maps = _stage(inputs)
    if "nc" not in _CACHE:
        _CACHE["nc"] = _build()
    nc = _CACHE["nc"]
    res = run_bass_kernel_spmd(nc, maps, core_ids=list(range(8)))
    return _assemble(res.results)


# version 10
# speedup vs baseline: 2.5989x; 2.5989x over previous
import numpy as np
import ml_dtypes
import concourse.bass as bass
import concourse.mybir as mybir
from concourse.bass import ds
from concourse.bass_utils import run_bass_kernel_spmd

F32 = mybir.dt.float32
BF16 = mybir.dt.bfloat16
I32 = mybir.dt.int32
U32 = mybir.dt.uint32
AF = mybir.ActivationFunctionType
ALU = mybir.AluOpType
AX = mybir.AxisListType
BF = ml_dtypes.bfloat16

D, S, NH, DKV = 2048, 4096, 16, 128
NE, NI, SI = 16, 1408, 2816
ICN = NI // 128
NDC = D // 128
KB = 1024

_DEBUG = False
_RACEDET = False
_STAGES = 99
_SUB = "lkvqrP"
_NPANEL = 16



def _stage(inp):
    x = np.asarray(inp["L0_attn_x"], dtype=np.float32)
    perm = np.concatenate([np.arange(0, 128, 2), np.arange(1, 128, 2)])

    Wq, Wk, Wv, Wo = (np.asarray(inp[f"L0_attn_W{p}"], np.float32) for p in "qkvo")
    Aq, Ak, Av, Ao = (np.asarray(inp[f"L0_attn_A{p}"], np.float32) for p in "qkvo")
    Bq, Bk, Bv, Bo = (np.asarray(inp[f"L0_attn_B{p}"], np.float32) for p in "qkvo")

    wqkvH, lb = {}, {}
    for G in (0, 1):
        mats, lbs = [], {}
        for p, W, Bm, permute in (("q", Wq, Bq, True), ("k", Wk, Bk, True),
                                  ("v", Wv, Bv, False)):
            rows = np.concatenate([h * 128 + (perm if permute else np.arange(128))
                                   for h in range(8 * G, 8 * G + 8)])
            mats.append(np.ascontiguousarray(W[rows].T).reshape(NDC, 128, 1024))
            lbs[p] = np.ascontiguousarray(
                Bm[:, rows, :].transpose(0, 2, 1).reshape(64, 1024))
        arr = np.stack(mats)
        arr = arr.reshape(3, NDC, 128, 2, 512).transpose(3, 2, 0, 1, 4)
        wqkvH[G] = np.ascontiguousarray(arr.reshape(2, 128, 3 * NDC * 512))
        lb[G] = lbs
    lbo = np.ascontiguousarray(Bo.transpose(0, 2, 1).reshape(64, 2048))

    woH = np.ascontiguousarray(
        Wo.T.reshape(NDC, 128, 2048).transpose(1, 0, 2).reshape(128, NDC * 2048))
    laqkvH = np.ascontiguousarray(
        np.stack([A.transpose(2, 0, 1).reshape(D, 64).reshape(NDC, 128, 64)
                  for A in (Aq, Ak, Av)])
        .transpose(2, 0, 1, 3).reshape(128, 3 * NDC * 64))
    laoH = np.ascontiguousarray(
        Ao.transpose(2, 0, 1).reshape(D, 64).reshape(NDC, 128, 64)
        .transpose(1, 0, 2).reshape(128, NDC * 64))
    gate = np.asarray(inp["L0_ffn_gate_w"], np.float32)
    gateH = np.ascontiguousarray(gate.T.reshape(NDC, 128, 16)
                                 .transpose(1, 0, 2).reshape(128, NDC * 16))

    xPH = {}
    for b in (0, 1):
        xt = np.ascontiguousarray(x[b].T).reshape(NDC, 128, 16, 256)
        xPH[b] = np.ascontiguousarray(
            xt.transpose(2, 1, 0, 3).reshape(16 * 128, NDC * 256))

    inv = 10000.0 ** (-np.arange(0, 128, 2, dtype=np.float64) / 128.0)
    angK = np.arange(S, dtype=np.float64)[None, :] * inv[:, None]
    ropekC = np.concatenate([np.cos(angK), np.cos(angK)], 0).astype(BF)
    ropekS = np.concatenate([-np.sin(angK), np.sin(angK)], 0).astype(BF)
    sc = 1.0 / np.sqrt(128.0)
    tri = np.where(np.arange(128)[None, :] < np.arange(128)[:, None], -1e9, 0.0)
    full = np.full((128, 128), -1e9)
    zero = np.zeros((128, 128))
    ropeqH, masksH = {}, {}
    for H in (0, 1):
        pq = np.concatenate([np.arange((2 * i + H) * 256, (2 * i + H) * 256 + 256)
                             for i in range(8)]).astype(np.float64)
        angQ = pq[None, :] * inv[:, None]
        ropeqH[H] = (np.concatenate([np.cos(angQ) * sc, np.cos(angQ) * sc], 0).astype(BF),
                     np.concatenate([-np.sin(angQ) * sc, np.sin(angQ) * sc], 0).astype(BF))
        m = np.zeros((16, 3, 128, 128), np.float32)
        for qc in range(16):
            m[qc] = (tri, full, full) if H == 0 else (zero, zero, tri)
        masksH[H] = np.ascontiguousarray(
            m.transpose(2, 0, 1, 3).reshape(128, 16 * 3 * 128)).astype(BF)

    w1 = np.asarray(inp["L0_ffn_w1"], np.float32)
    w3 = np.asarray(inp["L0_ffn_w3"], np.float32)
    w2 = np.asarray(inp["L0_ffn_w2"], np.float32)
    sw1 = np.asarray(inp["L0_ffn_sw1"], np.float32)
    sw3 = np.asarray(inp["L0_ffn_sw3"], np.float32)
    sw2 = np.asarray(inp["L0_ffn_sw2"], np.float32)

    def tile13(w):
        e = w.shape[0]
        icn = w.shape[1] // 128
        arr = w.reshape(e, icn, 128, NDC, 128).transpose(0, 1, 4, 3, 2)
        return np.ascontiguousarray(arr.reshape(e, icn, 128, D))

    def tile2(w):
        e = w.shape[0]
        icn = w.shape[2] // 128
        return np.ascontiguousarray(w.transpose(0, 2, 1).reshape(e, icn, 128, D))

    w1H, w3H, w2H = tile13(w1), tile13(w3), tile2(w2)
    sw1H = tile13(sw1[None]).reshape(2, ICN, 128, D)
    sw3H = tile13(sw3[None]).reshape(2, ICN, 128, D)
    sw2H = tile2(sw2[None]).reshape(2, ICN, 128, D)

    starts = np.asarray(inp["L0_attn_lora_start_indices"], np.int32).reshape(1, 4)
    arr64 = (np.arange(64) // 16).astype(np.float32).reshape(64, 1)
    nonce = np.array([[np.random.randint(1, 2 ** 31)]], dtype=np.uint32)

    maps = []
    for c in range(8):
        B, H, G = c // 4, (c // 2) % 2, c % 2
        xQH = np.ascontiguousarray(np.concatenate(
            [xPH[B][(2 * gi + H) * 128:(2 * gi + H) * 128 + 128] for gi in range(8)], 0))
        blocks = [4 * j + 2 * G + H for j in range(4)]
        xres = np.ascontiguousarray(
            np.concatenate([x[B, b * 256:(b + 1) * 256] for b in blocks], 0)
            .reshape(8, 128, 2048))
        maps.append({
            "xPH": xPH[B], "xQH": xQH, "xres": xres, "wqkv": wqkvH[G], "wo": woH,
            "laqkv": laqkvH, "lao": laoH,
            "lbq": lb[G]["q"], "lbk": lb[G]["k"], "lbv": lb[G]["v"], "lbo": lbo,
            "gateT": gateH, "ropekC": ropekC, "ropekS": ropekS,
            "ropeqC": ropeqH[H][0], "ropeqS": ropeqH[H][1],
            "masks": masksH[H], "starts": starts,
            "bval": np.array([[B + 0.5]], np.float32), "arr64": arr64,
            "nonce": nonce,
            "w1T": w1H, "w3T": w3H, "w2T": w2H,
            "sw1T": sw1H, "sw3T": sw3H, "sw2T": sw2H,
        })
        if _STAGES < 4:
            for k in ("w1T", "w3T", "w2T", "sw1T", "sw3T", "sw2T"):
                maps[-1].pop(k)
    return maps


def _assemble(results):
    out = np.empty((2, S, D), np.float32)
    for c in range(8):
        B, H, G = c // 4, (c // 2) % 2, c % 2
        o = np.asarray(results[c]["out"])
        for j in range(4):
            gb = 4 * j + 2 * G + H
            out[B, gb * 256:(gb + 1) * 256] = o[2 * j:2 * j + 2].reshape(256, 2048)
    return out



class _Pl:
    ENGS = ("s", "g", "t", "v", "a")
    ATTR = {"s": "sync", "g": "gpsimd", "t": "tensor", "v": "vector", "a": "scalar"}

    def __init__(self, nc):
        self.nc = nc
        self.q = {e: [] for e in self.ENGS}
        self.sem = {}
        for name in ("te", "ve", "ae", "ge"):
            self.sem[name] = [nc.alloc_semaphore(name=name), 0]
        self.hw = {e: {} for e in self.ENGS}
        self.war = {}

    def _getsem(self, name):
        if name not in self.sem:
            self.sem[name] = [self.nc.alloc_semaphore(name=name), 0]
        return self.sem[name]

    def _wait(self, eng, toks):
        byname = {}
        for t in toks:
            if t is None:
                continue
            sn, v = t
            if v > byname.get(sn, 0):
                byname[sn] = v
        for sn, v in byname.items():
            if eng == "t" and sn == "te":
                continue
            if self.hw[eng].get(sn, 0) >= v:
                continue
            self.hw[eng][sn] = v
            h = self._getsem(sn)[0]
            self.q[eng].append(lambda e, h=h, v=v: e.wait_ge(h, v))

    def emit(self, eng, fn, after=(), inc=None, dma=False):
        self._wait(eng, after)
        tok = None
        if inc is not None:
            rec = self._getsem(inc)
            rec[1] += 16 if dma else 1
            h, tgt, k = rec[0], rec[1], (16 if dma else 1)
            self.q[eng].append(lambda e, fn=fn, h=h, k=k: fn(e).then_inc(h, k))
            tok = (inc, tgt)
        else:
            self.q[eng].append(fn)
        return tok

    def dma(self, eng, out, in_, after=(), stream="misc"):
        return self.emit(eng, lambda e, o=out, i=in_: e.dma_start(out=o, in_=i),
                         after=after, inc=stream, dma=True)

    def mm(self, out, lhsT, rhs, start, stop, after=(), inc=False):
        return self.emit(
            "t", lambda e, o=out, l=lhsT, r=rhs, a=start, b=stop:
            e.matmul(o, l, r, start=a, stop=b),
            after=after, inc=("te" if inc else None))

    def use(self, *names):
        out = []
        for n in names:
            v = self.war.get(n)
            if v is None:
                continue
            out.extend(v if isinstance(v, list) else [v])
        return [t for t in out if t is not None]

    def setw(self, name, tok):
        if tok is not None:
            self.war[name] = tok

    def run(self, block):
        for name in self.ENGS:
            fns = self.q[name]
            if not fns:
                continue
            getattr(block, self.ATTR[name])(
                lambda eng, fns=fns: [f(eng) for f in fns])



def _build():
    nc = bass.Bass("TRN2", target_bir_lowering=False, debug=False,
                   detect_race_conditions=_RACEDET)

    def di(name, shape, dt=F32):
        return nc.dram_tensor(name, list(shape), dt, kind="ExternalInput").ap()

    io = {}
    io["xPH"] = di("xPH", [16 * 128, NDC * 256])
    io["xQH"] = di("xQH", [8 * 128, NDC * 256])
    io["xres"] = di("xres", [8, 128, 2048])
    io["wqkv"] = di("wqkv", [2, 128, 3 * NDC * 512])
    io["wo"] = di("wo", [128, NDC * 2048])
    io["laqkv"] = di("laqkv", [128, 3 * NDC * 64])
    io["lao"] = di("lao", [128, NDC * 64])
    for n in ("lbq", "lbk", "lbv"):
        io[n] = di(n, [64, 1024])
    io["lbo"] = di("lbo", [64, 2048])
    io["gateT"] = di("gateT", [128, NDC * 16])
    io["ropekC"] = di("ropekC", [128, S], BF16)
    io["ropekS"] = di("ropekS", [128, S], BF16)
    io["ropeqC"] = di("ropeqC", [128, 2048], BF16)
    io["ropeqS"] = di("ropeqS", [128, 2048], BF16)
    io["masks"] = di("masks", [128, 16 * 3 * 128], BF16)
    io["starts"] = di("starts", [1, 4], I32)
    io["bval"] = di("bval", [1, 1])
    io["arr64"] = di("arr64", [64, 1])
    io["nonce"] = di("nonce", [1, 1], U32)
    if _STAGES >= 4:
        io["w1T"] = di("w1T", [NE, ICN, 128, D])
        io["w3T"] = di("w3T", [NE, ICN, 128, D])
        io["w2T"] = di("w2T", [NE, ICN, 128, D])
        io["sw1T"] = di("sw1T", [2, ICN, 128, D])
        io["sw3T"] = di("sw3T", [2, ICN, 128, D])
        io["sw2T"] = di("sw2T", [2, ICN, 128, D])
    io["out"] = nc.dram_tensor("out", [8, 128, 2048], F32,
                               kind="ExternalOutput").ap()
    io["shp"] = nc.dram_tensor("shp", [2, 2048, 1024], BF16, kind="Internal",
                               addr_space="Shared").ap()
    io["shpf"] = io["shp"].rearrange("s q c -> (s q) c")
    io["shf"] = nc.dram_tensor("shf", [2, 64], U32, kind="Internal",
                               addr_space="Shared").ap()
    if _DEBUG:
        io["dbgq"] = nc.dram_tensor("dbgq", [128, 8192], BF16, kind="ExternalOutput").ap()
        io["dbgk"] = nc.dram_tensor("dbgk", [128, 4 * S], BF16, kind="ExternalOutput").ap()
        io["dbgv"] = nc.dram_tensor("dbgv", [128, 32 * 516], BF16, kind="ExternalOutput").ap()
        io["dbgattn"] = nc.dram_tensor("dbgattn", [8, 128, 2048], BF16, kind="ExternalOutput").ap()
        io["dbgx1"] = nc.dram_tensor("dbgx1", [8, 128, 2048], F32, kind="ExternalOutput").ap()
        io["dbggw"] = nc.dram_tensor("dbggw", [128, 128], F32, kind="ExternalOutput").ap()
        io["dbgh"] = nc.dram_tensor("dbgh", [128, ICN * 1024], BF16, kind="ExternalOutput").ap()

    sb = {}

    def ps_(name, shape, dt=F32):
        sb[name] = nc.alloc_psum_tensor(name, list(shape), dt)

    def pers(name, shape, dt):
        sb[name] = nc.alloc_sbuf_tensor(name, list(shape), dt)

    pers("ident", [128, 128], BF16)
    pers("ones_bf", [128, 1], BF16)
    pers("ones_f1", [1, 128], F32)
    pers("lao_sb", [128, NDC * 64], BF16)
    pers("lbo_sb", [64, 2048], BF16)
    pers("gate_sb", [128, NDC * 16], BF16)
    pers("wg_sb", [128, 8 * 16], F32)
    pers("gsc", [128, 48], F32)
    pers("sel_sb", [64, 1], F32)
    pers("arr64_sb", [64, 1], F32)
    pers("starts_f", [1, 8], F32)
    pers("starts_i", [1, 8], I32)
    pers("bval_sb", [1, 1], F32)
    pers("nonce_sb", [1, 64], U32)
    pers("flag_sb", [1, 64], U32)
    pers("tiny", [128, 16], F32)
    pers("t11", [1, 16], F32)
    pers("bias_h", [128, 1], F32)
    pers("epsb", [128, 1], F32)
    pers("rstd", [128, 4], F32)
    pers("ssum", [128, 4], F32)

    base = (nc.sbuf_base + 63) & ~63
    AR = 197 * KB
    assert base + AR <= nc.sbuf_top, (base, nc.sbuf_top, nc.sbuf_top - base)

    def at(name, off_kb, shape, dt):
        sb[name] = nc.alloc_sbuf_tensor_at(name, list(shape), dt,
                                           offset=base + int(off_kb * KB))

    at("b_xg0", 0, [128, NDC * 256], BF16)
    at("b_xg1", 8, [128, NDC * 256], BF16)
    at("b_xq", 16, [128, NDC * 256], BF16)
    at("b_low0", 24, [128, 1024], BF16)
    at("b_low1", 26, [128, 1024], BF16)
    at("b_qfm", 28, [128, 4 * 2048], BF16)
    at("b_kfm", 44, [128, 4 * S], BF16)
    at("b_v", 76, [128, 32 * 516], BF16)
    at("b_wqkv", 109, [128, 3 * NDC * 512], BF16)
    at("b_rkC", 157, [128, S], BF16)
    at("b_rkS", 165, [128, S], BF16)
    at("b_rqC", 173, [128, 2048], BF16)
    at("b_rqS", 177, [128, 2048], BF16)
    at("b_laqkv", 181, [128, 3 * NDC * 64], BF16)
    at("b_lb", 187, [64, 3 * 1024], BF16)
    at("b_us", 193, [128, 512], BF16)
    at("b_m1", 194, [128, 512], BF16)
    at("b_m2", 195, [128, 512], BF16)
    at("b_uu", 196, [128, 512], BF16)
    at("b_pt0", 109, [128, 1024], BF16)
    at("b_pt1", 111, [128, 1024], BF16)
    at("b_masks", 113, [128, 16 * 3 * 128], BF16)
    at("b_qsq", 125, [128, S], BF16)
    at("b_astg", 133, [128, 4 * 2048], BF16)
    at("d_wo", 0, [128, NDC * 2048], BF16)
    at("d_attnP", 64, [128, 4 * 2048], BF16)
    at("d_afm", 80, [128, 2048], BF16)
    at("d_xres", 84, [128, 2048], F32)
    at("d_lowo", 92, [64, 1024], BF16)
    at("d_sq", 0, [128, 2048], BF16)
    at("d_x1tm", 8, [128, 2048], BF16)
    at("x1fm", 96, [128, NDC * 1024], BF16)
    at("res", 128, [128, 8 * 2048], F32)
    at("e_h", 0, [128, ICN * 1024], BF16)
    at("e_w1r0", 22, [128, D], BF16)
    at("e_w1r1", 26, [128, D], BF16)
    at("e_w3r0", 30, [128, D], BF16)
    at("e_w3r1", 34, [128, D], BF16)
    at("e_w1r2", 92, [128, D], BF16)
    at("e_w3r2", 193, [128, D], BF16)
    at("e_w2", 38, [128, ICN * 2048], BF16)
    at("e_t1", 82, [128, 1024], BF16)
    at("e_oscr", 84, [128, 2048], F32)

    ps_("psA", [128, 1024])
    ps_("psB", [128, 1024])
    ps_("psC", [128, 512])
    ps_("psD", [128, 512])
    ps_("psE", [128, 512])
    ps_("psT", [128, 1024], BF16)

    sb["psTf"] = sb["psT"][:].bitcast(F32)

    P = _Pl(nc)
    with nc.Block() as block:
        _emit(P, io, sb)
        P.run(block)
    return nc


def _emit(P, io, sb):
    nc = P.nc
    psA, psB, psC, psD, psE, psT = (sb[k] for k in
                                    ("psA", "psB", "psC", "psD", "psE", "psT"))

    RT = {}

    def _init_rt(e):
        pid = e.partition_id()
        RT["g"] = e.to_reg(e.compute_val(pid & 1))
        RT["h128"] = e.to_reg(e.compute_val((pid & 2) * 64))
        RT["g256"] = e.to_reg(e.compute_val((pid & 1) * 256))
        RT["g2048"] = e.to_reg(e.compute_val((pid & 1) * 2048))
        RT["scr"] = e.alloc_register("dynoff")
    P.emit("g", _init_rt)

    def _dyn(e, basekey, const, maxv):
        e.reg_alu(RT["scr"], RT[basekey], const, ALU.add)
        return e.snap(RT["scr"], min_val=0, max_val=maxv)

    def _dynfree(e, off):
        try:
            e.free_register(off.val)
        except Exception:
            pass

    RTS = {}

    def _init_rts(e):
        pid = e.partition_id()
        RTS["g256"] = e.to_reg(e.compute_val((pid & 1) * 256))
        RTS["scr"] = e.alloc_register("dynoffs")
    P.emit("s", _init_rts)

    def _dyn_s(e, const, maxv):
        e.reg_alu(RTS["scr"], RTS["g256"], const, ALU.add)
        return e.snap(RTS["scr"], min_val=0, max_val=maxv)

    pre = []
    pre.append(P.dma("g", sb["starts_i"][0:1, 0:4], io["starts"], stream="pre"))
    pre.append(P.dma("g", sb["bval_sb"][:], io["bval"], stream="pre"))
    pre.append(P.dma("g", sb["arr64_sb"][:], io["arr64"], stream="pre"))
    tnonce = P.dma("g", sb["nonce_sb"][0:1, 0:1], io["nonce"], stream="nonce")
    pre.append(tnonce)
    pre.append(P.dma("g", sb["lao_sb"][:], io["lao"], stream="pre"))
    pre.append(P.dma("g", sb["lbo_sb"][:], io["lbo"], stream="pre"))
    pre.append(P.dma("g", sb["gate_sb"][:], io["gateT"], stream="pre"))

    tms = P.emit("g", lambda e: e.memset(sb["ident"][:], 0.0), inc="ge")
    tid = P.emit("g", lambda e: e.affine_select(
        out=sb["ident"][:], in_=sb["ident"][:], compare_op=ALU.not_equal,
        fill=1.0, base=0, pattern=[[-1, 128]], channel_multiplier=1),
        after=[tms], inc="ge")
    tob = P.emit("v", lambda e: e.memset(sb["ones_bf"][:], 1.0), inc="ve")
    teps = P.emit("v", lambda e: e.memset(sb["epsb"][:], 1e-5), inc="ve")
    tof = P.emit("v", lambda e: e.memset(sb["ones_f1"][:], 1.0), inc="ve")

    t0 = P.emit("v", lambda e: e.tensor_copy(sb["starts_f"][0:1, 0:4],
                                             sb["starts_i"][0:1, 0:4]),
                after=pre, inc="ve")
    t1 = P.emit("v", lambda e: e.tensor_scalar(
        sb["starts_f"][0:1, 4:8], sb["starts_f"][0:1, 0:4],
        sb["bval_sb"][0:1, 0:1], None, ALU.is_lt), after=[t0], inc="ve")
    t2 = P.emit("v", lambda e: e.reduce_sum(
        sb["t11"][0:1, 0:1], sb["starts_f"][0:1, 4:8], axis=AX.X),
        after=[t1], inc="ve")
    t3 = P.emit("v", lambda e: e.tensor_scalar(
        sb["t11"][0:1, 1:2], sb["t11"][0:1, 0:1], -1.0, 0.0, ALU.add, ALU.max),
        after=[t2], inc="ve")
    t4 = P.emit("v", lambda e: e.tensor_scalar(
        sb["t11"][0:1, 2:3], sb["t11"][0:1, 1:2], 3.0, None, ALU.min),
        after=[t3], inc="ve")
    t5 = P.mm(psE[0:64, 0:1], sb["ones_f1"][0:1, 0:64], sb["t11"][0:1, 2:3],
              True, True, after=[t4, tof], inc=True)
    tsel = P.emit("v", lambda e: e.tensor_tensor(
        sb["sel_sb"][:], sb["arr64_sb"][:], psE[0:64, 0:1], ALU.is_equal),
        after=[t5] + pre, inc="ve")
    P.setw("psE", tsel)

    if _STAGES < 1:
        _finish(P, io, sb, [])
        return

    attn_dmas = []
    proj_ps = [(psA, 0), (psA, 512), (psB, 0), (psB, 512)]

    for w in range(2):
        wv = []
        for i in range(4):
            wv.append(P.dma("g", sb["b_wqkv"][:, i * 6144:(i + 1) * 6144],
                            io["wqkv"][w, :, i * 6144:(i + 1) * 6144],
                            after=P.use("swap"), stream="wv"))
        wv.append(P.dma("g", sb["b_rkC"][:], io["ropekC"], after=P.use("swap"), stream="wv"))
        wv.append(P.dma("g", sb["b_rkS"][:], io["ropekS"], after=P.use("swap"), stream="wv"))
        wv.append(P.dma("g", sb["b_rqC"][:], io["ropeqC"], after=P.use("swap"), stream="wv"))
        wv.append(P.dma("g", sb["b_rqS"][:], io["ropeqS"], after=P.use("swap"), stream="wv"))
        wv.append(P.dma("g", sb["b_laqkv"][:], io["laqkv"], after=P.use("swap"), stream="wv"))
        wv.append(P.dma("g", sb["b_lb"][0:64, 0:1024], io["lbq"], after=P.use("swap"), stream="wv"))
        wv.append(P.dma("g", sb["b_lb"][0:64, 1024:2048], io["lbk"], after=P.use("swap"), stream="wv"))
        wv.append(P.dma("g", sb["b_lb"][0:64, 2048:3072], io["lbv"], after=P.use("swap"), stream="wv"))
        wv.append(tsel)
        tvz = P.emit("v", lambda e: e.memset(sb["b_v"][:], 1.0),
                     after=P.use("b_v", "dbgv", "swap"), inc="ve")

        kfm_t, qfm_t, vt_t = [], [], []
        psi = 0
        for pn in range(_NPANEL):
            bi = pn % 2
            xg = sb[f"b_xg{bi}"]
            txg = P.dma("g", xg[:], io["xPH"][pn * 128:(pn + 1) * 128, :],
                        after=P.use(f"b_xg{bi}"), stream=f"xg{bi}")
            if pn % 2 == 0:
                gi = pn // 2

                txq = P.dma("g", sb["b_xq"][:],
                            io["xQH"][gi * 128:(gi + 1) * 128, :],
                            after=P.use("b_xq"), stream="xq")

            lowt = sb[f"b_low{bi}"]
            lw = {}
            specs = []
            if "l" in _SUB:
                specs = [("k", xg, txg, 1, psC, 0), ("v", xg, txg, 2, psD, 256)]
                if pn % 2 == 0:
                    specs.append(("q", sb["b_xq"], txq, 0, psE, 512))
            for name, src, stok, Pi, ps, col in specs:
                pn_tok = None
                for dc in range(NDC):
                    pn_tok = P.mm(
                        ps[0:64, 0:256],
                        sb["b_laqkv"][:, Pi * 1024 + dc * 64: Pi * 1024 + dc * 64 + 64],
                        src[:, dc * 256:(dc + 1) * 256],
                        dc == 0, dc == NDC - 1,
                        after=[stok] + wv + P.use("ps_low_" + name),
                        inc=(dc == NDC - 1))
                ev = P.emit("v", lambda e, d=lowt[0:64, col:col + 256], p=ps:
                            e.tensor_scalar(d, p[0:64, 0:256],
                                            sb["sel_sb"][0:64, 0:1], None, ALU.mult),
                            after=[pn_tok, tsel] + P.use(f"b_low{bi}"), inc="ve")
                P.setw("ps_low_" + name, ev)
                lw[name] = ev

            cons = []
            for oc in (range(4) if "k" in _SUB else []):
                ps, po = proj_ps[psi]
                pw = P.use(f"proj{psi}")
                for dc in range(NDC):
                    P.mm(ps[:, po:po + 256],
                         sb["b_wqkv"][:, 8192 + dc * 512 + oc * 128:
                                      8192 + dc * 512 + oc * 128 + 128],
                         xg[:, dc * 256:(dc + 1) * 256],
                         dc == 0, False, after=[txg] + wv + pw)
                tdel = P.mm(ps[:, po:po + 256],
                            sb["b_lb"][:, 1024 + w * 512 + oc * 128:
                                       1024 + w * 512 + oc * 128 + 128],
                            lowt[0:64, 0:256], False, True,
                            after=[lw["k"]], inc=True)
                if "r" not in _SUB:
                    co = oc * S + pn * 256
                    tk1 = P.emit("a", lambda e, p=ps, po=po, co=co: e.copy(
                        sb["b_kfm"][:, co:co + 256], p[:, po:po + 256]),
                        after=[tdel] + P.use(f"proj{psi}_e"), inc="ae")
                    kfm_t += [tk1]
                    P.setw(f"proj{psi}", tk1)
                    cons.append(tk1)
                    psi = (psi + 1) % 4
                    continue
                if "Z" in _SUB:
                    tuu = tdel
                else:
                    tuu = P.emit("a", lambda e, p=ps, po=po: e.copy(
                        sb["b_uu"][:, 0:256], p[:, po:po + 256]),
                        after=[tdel] + P.use("b_uu"), inc="ae")
                if "Y" in _SUB:
                    tc2 = tuu
                elif "X" in _SUB:
                    tc2 = P.emit("a", lambda e: e.copy(
                        sb["b_us"][:, 0:256], sb["b_uu"][:, 0:256]),
                        after=[tuu] + P.use("b_us"), inc="ae")
                else:
                    P.emit("a", lambda e: e.copy(
                        sb["b_us"][0:64, 0:256], sb["b_uu"][64:128, 0:256]),
                        after=[tuu] + P.use("b_us"))
                    tc2 = P.emit("a", lambda e: e.copy(
                        sb["b_us"][64:128, 0:256], sb["b_uu"][0:64, 0:256]),
                        inc="ae")
                tc1 = tc2
                tm1 = P.emit("v", lambda e, pn=pn: e.tensor_tensor(
                    sb["b_m1"][:, 0:256], sb["b_uu"][:, 0:256],
                    sb["b_rkC"][:, pn * 256:(pn + 1) * 256], ALU.mult),
                    after=[tuu] + P.use("b_m1"), inc="ve")
                if "1" in _SUB:
                    co = oc * S + pn * 256
                    tk1 = P.emit("v", lambda e, co=co: e.tensor_copy(
                        sb["b_kfm"][:, co:co + 256], sb["b_m1"][:, 0:256]),
                        after=[tm1, tc1, tc2], inc="ve")
                    tm2 = tk1
                else:
                    tm2 = P.emit("g" if "P" in _SUB else "v",
                                 lambda e, pn=pn: e.tensor_tensor(
                        sb["b_m2"][:, 0:256], sb["b_us"][:, 0:256],
                        sb["b_rkS"][:, pn * 256:(pn + 1) * 256], ALU.mult),
                        after=[tc1, tc2] + P.use("b_m2"),
                        inc=("ge" if "P" in _SUB else "ve"))
                    co = oc * S + pn * 256
                    tk1 = P.emit("v", lambda e, co=co: e.tensor_tensor(
                        sb["b_kfm"][:, co:co + 256], sb["b_m1"][:, 0:256],
                        sb["b_m2"][:, 0:256], ALU.add),
                        after=[tm1, tm2], inc="ve")
                kfm_t += [tk1]
                P.setw(f"proj{psi}", [tuu])
                P.setw("b_uu", [tc2, tm1])
                P.setw("b_us", tm2)
                P.setw("b_m1", tk1)
                P.setw("b_m2", tk1)
                cons.append(tm1)
                cons.append(tuu)
                psi = (psi + 1) % 4

            for tcc in (range(2) if "v" in _SUB else []):
                ps, po = proj_ps[psi]
                pw = P.use(f"proj{psi}")
                for dc in range(NDC):
                    P.mm(ps[:, po:po + 512],
                         xg[:, dc * 256 + tcc * 128: dc * 256 + tcc * 128 + 128],
                         sb["b_wqkv"][:, 16384 + dc * 512: 16384 + dc * 512 + 512],
                         dc == 0, False, after=[txg] + wv + pw)
                tdel = P.mm(ps[:, po:po + 512],
                            lowt[0:64, 256 + tcc * 128: 256 + tcc * 128 + 128],
                            sb["b_lb"][:, 2048 + w * 512: 2048 + w * 512 + 512],
                            False, True, after=[lw["v"]], inc=True)
                kc = pn * 2 + tcc
                tv = P.emit("a", lambda e, p=ps, po=po, kc=kc: e.copy(
                    sb["b_v"][:].rearrange("p (k h c) -> p k h c", k=32, h=4)[
                        :, kc, :, 0:128],
                    p[:, po:po + 512].rearrange("p (h c) -> p h c", h=4)),
                    after=[tdel, tvz], inc="ae")
                vt_t.append(tv)
                P.setw(f"proj{psi}", tv)
                cons.append(tv)
                psi = (psi + 1) % 4

            if pn % 2 == 0 and "q" in _SUB:
                gi = pn // 2
                for oc in range(4):
                    ps, po = proj_ps[psi]
                    pw = P.use(f"proj{psi}")
                    for dc in range(NDC):
                        P.mm(ps[:, po:po + 256],
                             sb["b_wqkv"][:, dc * 512 + oc * 128:
                                          dc * 512 + oc * 128 + 128],
                             sb["b_xq"][:, dc * 256:(dc + 1) * 256],
                             dc == 0, False, after=[txq] + wv + pw)
                    tdel = P.mm(ps[:, po:po + 256],
                                sb["b_lb"][:, w * 512 + oc * 128:
                                           w * 512 + oc * 128 + 128],
                                lowt[0:64, 512:768], False, True,
                                after=[lw["q"]], inc=True)
                    tuu = P.emit("a", lambda e, p=ps, po=po: e.copy(
                        sb["b_uu"][:, 256:512], p[:, po:po + 256]),
                        after=[tdel] + P.use("b_uuq"), inc="ae")
                    P.emit("a", lambda e: e.copy(
                        sb["b_us"][0:64, 256:512], sb["b_uu"][64:128, 256:512]),
                        after=[tuu] + P.use("b_usq"))
                    tc2 = P.emit("a", lambda e: e.copy(
                        sb["b_us"][64:128, 256:512], sb["b_uu"][0:64, 256:512]),
                        inc="ae")
                    tc1 = tc2
                    tm1 = P.emit("v", lambda e, gi=gi: e.tensor_tensor(
                        sb["b_m1"][:, 256:512], sb["b_uu"][:, 256:512],
                        sb["b_rqC"][:, gi * 256:(gi + 1) * 256], ALU.mult),
                        after=[tuu] + P.use("b_m1q"), inc="ve")
                    tm2 = P.emit("g", lambda e, gi=gi: e.tensor_tensor(
                        sb["b_m2"][:, 256:512], sb["b_us"][:, 256:512],
                        sb["b_rqS"][:, gi * 256:(gi + 1) * 256], ALU.mult),
                        after=[tc1, tc2] + P.use("b_m2q"), inc="ge")
                    co = oc * 2048 + gi * 256
                    tq1 = P.emit("v", lambda e, co=co: e.tensor_tensor(
                        sb["b_qfm"][:, co:co + 256], sb["b_m1"][:, 256:512],
                        sb["b_m2"][:, 256:512], ALU.add),
                        after=[tm1, tm2] + P.use("dbgqk"), inc="ve")
                    qfm_t += [tq1]
                    P.setw(f"proj{psi}", [tuu])
                    P.setw("b_uuq", [tc2, tm1])
                    P.setw("b_usq", tm2)
                    P.setw("b_m1q", tq1)
                    P.setw("b_m2q", tq1)
                    cons.append(tm1)
                    cons.append(tuu)
                    psi = (psi + 1) % 4
                P.setw("b_xq", _lastl(cons))
            P.setw(f"b_xg{bi}", _lastl(cons))
            P.setw(f"b_low{bi}", _lastl(cons))

        proj_done = kfm_t + qfm_t + vt_t
        if _DEBUG and w == 0:
            td1 = P.dma("s", io["dbgq"], sb["b_qfm"][:], after=proj_done, stream="dbg")
            td2 = P.dma("s", io["dbgk"], sb["b_kfm"][:], after=proj_done, stream="dbg")
            td3 = P.dma("s", io["dbgv"], sb["b_v"][:], after=proj_done, stream="dbg")
            P.setw("dbgv", [td3])
            P.setw("dbgqk", [td1, td2])
        if _STAGES < 2:
            _finish(P, io, sb, [])
            return

        tmask = P.dma("s", sb["b_masks"][:], io["masks"], after=proj_done, stream="mask")

        wave_ev = []
        stg = sb["b_astg"]
        for hl in range(4):
            qcol, kcol = hl * 2048, hl * S
            tq2m = P.emit("v", lambda e, qcol=qcol: e.tensor_tensor(
                sb["b_qsq"][:, 0:2048], sb["b_qfm"][:, qcol:qcol + 2048],
                sb["b_qfm"][:, qcol:qcol + 2048], ALU.mult),
                after=proj_done + P.use("b_qsq"), inc="ve")
            nt = []
            prev = tq2m
            for blk, nblk in ((0, 4), (1, 8)):
                if blk == 1:
                    prev = P.emit("v", lambda e, kcol=kcol: e.tensor_tensor(
                        sb["b_qsq"][:, 0:4096], sb["b_kfm"][:, kcol:kcol + S],
                        sb["b_kfm"][:, kcol:kcol + S], ALU.mult),
                        after=nt + [prev], inc="ve")
                for r in range(nblk):
                    tm1 = P.mm(psC[0:1, 0:512], sb["ones_bf"][:],
                               sb["b_qsq"][:, r * 512:(r + 1) * 512], True, True,
                               after=[prev, tob] + P.use("psC", "ps_av0"),
                               inc=True)
                    tr = P.emit("v", lambda e: e.reduce_max(
                        sb["t11"][0:1, 8:9], psC[0:1, 0:512], axis=AX.X),
                        after=[tm1], inc="ve")
                    P.setw("psC", tr)
                    if r == 0:
                        ta = P.emit("v", lambda e, o=blk: e.tensor_copy(
                            sb["t11"][0:1, 4 + o:5 + o], sb["t11"][0:1, 8:9]),
                            after=[tr], inc="ve")
                    else:
                        ta = P.emit("v", lambda e, o=blk: e.tensor_tensor(
                            sb["t11"][0:1, 4 + o:5 + o], sb["t11"][0:1, 4 + o:5 + o],
                            sb["t11"][0:1, 8:9], ALU.max), after=[tr], inc="ve")
                    prev = ta
                    nt.append(ta)
                    P.setw("b_qsq", tm1)
            tpr = P.emit("v", lambda e: e.tensor_tensor(
                sb["t11"][0:1, 6:7], sb["t11"][0:1, 4:5], sb["t11"][0:1, 5:6],
                ALU.mult), after=nt, inc="ve")
            tsq = P.emit("a", lambda e: e.activation(
                sb["t11"][0:1, 7:8], sb["t11"][0:1, 6:7], AF.Sqrt),
                after=[tpr], inc="ae")
            tng = P.emit("v", lambda e: e.tensor_scalar(
                sb["t11"][0:1, 9:10], sb["t11"][0:1, 7:8], -1.0, None, ALU.mult),
                after=[tsq], inc="ve")
            tbc = P.mm(psD[0:128, 0:1], sb["ones_f1"][0:1, 0:128],
                       sb["t11"][0:1, 9:10], True, True,
                       after=[tng, tof] + P.use("psD", "ps_av1"),
                       inc=True)
            tbias = P.emit("a", lambda e: e.copy(sb["bias_h"][:], psD[0:128, 0:1]),
                           after=[tbc] + P.use("bias_h"), inc="ae")
            P.setw("psD", tbias)

            for qg in range(4):
                av_done = []
                for kcp in range(4 * qg + 4):
                    pt = sb[f"b_pt{kcp % 2}"]
                    ps = (psA, psB)[kcp % 2]
                    mm_t = []
                    for kk in range(2):
                        kc = kcp * 2 + kk
                        tmm = P.mm(ps[:, kk * 512:kk * 512 + 512],
                                   sb["b_kfm"][:, kcol + kc * 128:
                                               kcol + kc * 128 + 128],
                                   sb["b_qfm"][:, qcol + qg * 512:
                                               qcol + qg * 512 + 512],
                                   True, True,
                                   after=[tmask, tbias] + proj_done +
                                   P.use(f"ps_sc{kcp % 2}"), inc=True)
                        mm_t.append(tmm)
                    mk_t = [[], []]
                    for kk in range(2):
                        kc = kcp * 2 + kk
                        for qq in range(4):
                            qc = qg * 4 + qq
                            d0 = 4 * (qc // 2) + (qc % 2)
                            sl = (ps, kk * 512 + qq * 128)
                            if d0 <= kc <= d0 + 2:
                                j = kc - d0
                                mk_t[kk].append(P.emit(
                                    "v", lambda e, s=sl, qc=qc, j=j: e.tensor_tensor(
                                        s[0][:, s[1]:s[1] + 128],
                                        s[0][:, s[1]:s[1] + 128],
                                        sb["b_masks"][:, (qc * 3 + j) * 128:
                                                      (qc * 3 + j) * 128 + 128],
                                        ALU.add),
                                    after=[mm_t[kk], tmask], inc="ve"))
                            elif kc > d0 + 2:
                                mk_t[kk].append(P.emit(
                                    "v", lambda e, s=sl: e.memset(
                                        s[0][:, s[1]:s[1] + 128], -1e9),
                                    after=[mm_t[kk]], inc="ve"))
                    texps = []
                    for kk in range(2):
                        texps.append(P.emit(
                            "a", lambda e, pt=pt, ps=ps, kk=kk: e.activation(
                                pt[:, kk * 512:kk * 512 + 512],
                                ps[:, kk * 512:kk * 512 + 512],
                                AF.Exp, bias=sb["bias_h"][0:128, 0:1]),
                            after=[mm_t[kk]] + mk_t[kk] + [tbias] +
                            P.use(f"b_pt{kcp % 2}"), inc="ae"))
                    texp = texps[1]
                    P.setw(f"ps_sc{kcp % 2}", [texps[0], texps[1]])
                    P.setw("bias_h", texp)
                    last_av = None
                    for kk in range(2):
                        kc = kcp * 2 + kk
                        for qq in range(4):
                            qc = qg * 4 + qq
                            d0 = 4 * (qc // 2) + (qc % 2)
                            if kc > d0 + 2:
                                continue
                            avp = (psC, psD, psE, sb["psTf"])[qq]
                            avs = avp[:, 0:129] if qq < 3 else avp[:, 0:129]
                            tav = P.mm(avs,
                                       pt[:, kk * 512 + qq * 128:
                                          kk * 512 + qq * 128 + 128],
                                       sb["b_v"][:].rearrange(
                                           "p (k x) -> p k x", k=32)[
                                           :, kc, hl * 129:hl * 129 + 129],
                                       kc == 0, kc == d0 + 2,
                                       after=[texps[kk], tvz] + vt_t +
                                       P.use(f"ps_av{qq}", "psC", "psD", "psE",
                                             "psT"),
                                       inc=(kc == d0 + 2))
                            if tav:
                                av_done.append(tav)
                                last_av = tav
                    P.setw(f"b_pt{kcp % 2}", last_av)
                rc_t = []
                for qq in range(4):
                    avp = (psC, psD, psE, sb["psTf"])[qq]
                    rc_t.append(P.emit("v", lambda e, avp=avp, qq=qq: e.reciprocal(
                        sb["rstd"][:, qq:qq + 1], avp[:, 128:129]),
                        after=av_done + P.use(f"rstd{qq}"), inc="ve"))
                ev_t = []
                for qq in range(4):
                    avp = (psC, psD, psE, sb["psTf"])[qq]
                    co = (qg * 4 + qq) * 512 + hl * 128
                    tev = P.emit("v", lambda e, avp=avp, qq=qq, co=co:
                                 e.tensor_scalar(
                                     stg[:, co:co + 128],
                                     avp[:, 0:128],
                                     sb["rstd"][:, qq:qq + 1], None, ALU.mult),
                                 after=rc_t + P.use("b_astg"), inc="ve")
                    ev_t.append(tev)
                    P.setw(f"ps_av{qq}", tev)
                    P.setw(f"rstd{qq}", tev)
                wave_ev += ev_t

        def attn_dma(e, w=w):
            off = _dyn(e, "g2048", 0, 2048)
            dst = io["shpf"][ds(off, 2048), w * 512:w * 512 + 512]
            inst = e.dma_start(
                out=dst.rearrange("(a p) c -> p a c", p=128),
                in_=stg[:].rearrange("p (a c) -> p a c", a=16))
            _dynfree(e, off)
            return inst
        ta = P.emit("g", attn_dma, after=wave_ev, inc="attnw", dma=True)
        attn_dmas.append(ta)
        P.setw("b_astg", ta)
        P.setw("swap", _lastl(attn_dmas))
        P.setw("b_v", _lastl(attn_dmas))
        P.setw("b_qsq", _lastl(attn_dmas))

    if _STAGES < 3:
        _finish(P, io, sb, [])
        return

    wo_ld = []
    for i in range(8):
        wo_ld.append(P.dma("g", sb["d_wo"][:, i * 4096:(i + 1) * 4096],
                           io["wo"][:, i * 4096:(i + 1) * 4096],
                           after=attn_dmas, stream="wo"))


    def _flagw(e):
        off = _dyn(e, "g", 0, 1)
        inst = e.dma_start(out=io["shf"][ds(off, 1), 0:1],
                           in_=sb["nonce_sb"][0:1, 0:1])
        _dynfree(e, off)
        return inst
    tflag = P.emit("g", _flagw, after=attn_dmas + [tnonce], inc="flag", dma=True)

    ld_base = 0
    ld_h = P._getsem("pol")[0]

    def poll(e, ld_base=ld_base, ld_h=ld_h):
        nv = e.value_load(sb["nonce_sb"][0:1, 0:1])
        other = e.compute_val((e.partition_id() & 1) != 1)
        cr = e.alloc_register("pollc")
        e.reg_mov(cr, ld_base)

        def cond():
            e.dma_start(out=sb["flag_sb"][0:1, 0:1],
                        in_=io["shf"][ds(other, 1), 0:1]).then_inc(ld_h, 16)
            e.reg_add(cr, cr, 16)
            e.wait_ge(ld_h, e.snap(cr))
            f = e.value_load(sb["flag_sb"][0:1, 0:1])
            return f != nv
        with e.While(cond):
            pass
    P.emit("g", poll, after=[tflag])
    tpoll = P.emit("g", lambda e: e.nop(), inc="ge")

    res_t = []
    shpb = io["shpf"].rearrange("(b r) c -> b r c", r=512)
    for g4 in range(2):
        attP = sb["d_attnP"]

        def rdj(e, g4=g4, slot=0, jj=0):
            bb = _dyn_s(e, 0, 256)
            srcv = shpb[slot * 4 + 2 * g4 + jj,
                        ds(bb, 256), :].rearrange("(s p) c -> p s c", p=128)
            dstv = attP[:].rearrange("p (t c) -> p t c", t=4)[
                :, 2 * jj:2 * jj + 2, slot * 1024:slot * 1024 + 1024]
            inst = e.dma_start(out=dstv, in_=srcv)
            _dynfree(e, bb)
            return inst
        rd_t = []
        for slot in range(2):
            for jj in range(2):
                rd_t.append(P.emit(
                    "s", lambda e, f=rdj, slot=slot, jj=jj: f(e, slot=slot, jj=jj),
                    after=[tpoll] + P.use("d_attnP"), inc="rd", dma=True))
        ta0 = ta1 = rd_t[-1]

        grp_cons = []
        for sub in range(4):
            tc = 4 * g4 + sub
            att = attP[:, sub * 2048:(sub + 1) * 2048]
            txr = P.dma("a", sb["d_xres"][:], io["xres"][tc],
                        after=P.use("d_xres"), stream="xres")
            afm = sb["d_afm"]
            tr_t = []
            for fq in range(4):
                tt = None
                for f2 in range(4):
                    fc = fq * 4 + f2
                    tt = P.emit("t", lambda e, fc=fc, f2=f2, att=att: e.transpose(
                        psT[:, f2 * 128:f2 * 128 + 128],
                        att[:, fc * 128:fc * 128 + 128], sb["ident"][:]),
                        after=[ta0, ta1, tid] + P.use("psT", "d_afm"),
                        inc=("te" if f2 == 3 else None))
                tev = P.emit("a", lambda e, fq=fq, afm=afm: e.copy(
                    afm[:, fq * 512:fq * 512 + 512], psT[:, 0:512]),
                    after=[tt] + P.use("d_afm"), inc="ae")
                P.setw("psT", tev)
                tr_t.append(tev)
            grp_cons.append(tr_t[-1])

            tlo_mm = None
            for fc in range(NDC):
                tlo_mm = P.mm(psE[0:64, 0:128],
                              sb["lao_sb"][:, fc * 64:fc * 64 + 64],
                              afm[:, fc * 128:fc * 128 + 128],
                              fc == 0, fc == NDC - 1,
                              after=tr_t + P.use("psE"), inc=(fc == NDC - 1))
            tlo = P.emit("v", lambda e, tc=tc: e.tensor_scalar(
                sb["d_lowo"][0:64, tc * 128:tc * 128 + 128], psE[0:64, 0:128],
                sb["sel_sb"][0:64, 0:1], None, ALU.mult),
                after=[tlo_mm, tsel], inc="ve")
            P.setw("psE", tlo)

            for fc in range(NDC):
                for og in range(4):
                    ps = (psA, psB)[og // 2]
                    P.mm(ps[:, (og % 2) * 512:(og % 2) * 512 + 512],
                         afm[:, fc * 128:fc * 128 + 128],
                         sb["d_wo"][:, fc * 2048 + og * 512:
                                    fc * 2048 + og * 512 + 512],
                         fc == 0, False,
                         after=wo_ld + tr_t + P.use(f"ps_wo{og}"))
            del_t = []
            for og in range(4):
                ps = (psA, psB)[og // 2]
                del_t.append(P.mm(ps[:, (og % 2) * 512:(og % 2) * 512 + 512],
                                  sb["d_lowo"][0:64, tc * 128:tc * 128 + 128],
                                  sb["lbo_sb"][:, og * 512:og * 512 + 512],
                                  False, True, after=[tlo], inc=True))
            ev_last = None
            for og in range(4):
                ps = (psA, psB)[og // 2]
                tev = P.emit("v", lambda e, ps=ps, og=og, tc=tc: e.tensor_tensor(
                    sb["res"][:, tc * 2048 + og * 512: tc * 2048 + og * 512 + 512],
                    ps[:, (og % 2) * 512:(og % 2) * 512 + 512],
                    sb["d_xres"][:, og * 512:og * 512 + 512], ALU.add),
                    after=del_t + [txr], inc="ve")
                P.setw(f"ps_wo{og}", tev)
                res_t.append(tev)
                ev_last = tev
            P.setw("d_afm", ev_last)
            P.setw("d_xres", ev_last)
            if _DEBUG:
                P.dma("s", io["dbgattn"][tc], att, after=tr_t, stream="dbg")
        P.setw("d_attnP", _lastl(grp_cons))

    x1_t = []
    for tc in range(8):
        rs = sb["res"][:, tc * 2048:(tc + 1) * 2048]
        tm = P.emit("v", lambda e, rs=rs: e.reduce_sum(
            sb["tiny"][:, 0:1], rs, axis=AX.X), after=res_t + P.use("ln1"),
            inc="ve")
        tm2 = P.emit("v", lambda e: e.tensor_scalar(
            sb["tiny"][:, 1:2], sb["tiny"][:, 0:1], 1.0 / 2048.0, None, ALU.mult),
            after=[tm], inc="ve")
        tsb = P.emit("v", lambda e, rs=rs: e.tensor_scalar(
            rs, rs, sb["tiny"][:, 1:2], None, ALU.subtract),
            after=[tm2], inc="ve")
        tq = P.emit("a", lambda e, rs=rs: e.activation(
            sb["d_sq"][:], rs, AF.Square, accum_out=sb["ssum"][:, 0:1]),
            after=[tsb] + P.use("d_sq"), inc="ae")
        P.setw("d_sq", tq)
        tsd = P.emit("a", lambda e: e.activation(
            sb["tiny"][:, 2:3], sb["ssum"][:, 0:1], AF.Sqrt,
            bias=sb["epsb"][:, 0:1], scale=1.0 / 2048.0),
            after=[tq, teps], inc="ae")
        trc = P.emit("v", lambda e: e.reciprocal(
            sb["tiny"][:, 3:4], sb["tiny"][:, 2:3]), after=[tsd], inc="ve")
        tml = P.emit("v", lambda e, rs=rs: e.tensor_scalar(
            rs, rs, sb["tiny"][:, 3:4], None, ALU.mult), after=[trc], inc="ve")
        tct = P.emit("a", lambda e, rs=rs: e.copy(sb["d_x1tm"][:], rs),
                     after=[tml] + P.use("d_x1tm"), inc="ae")
        P.setw("ln1", tct)
        for fq in range(4):
            tt = None
            for f2 in range(4):
                dc = fq * 4 + f2
                tt = P.emit("t", lambda e, dc=dc, f2=f2: e.transpose(
                    psT[:, f2 * 128:f2 * 128 + 128],
                    sb["d_x1tm"][:, dc * 128:dc * 128 + 128], sb["ident"][:]),
                    after=[tct, tid] + P.use("psT"),
                    inc=("te" if f2 == 3 else None))
            tev = P.emit("a", lambda e, fq=fq, tc=tc: e.copy(
                sb["x1fm"][:].rearrange("p (dc t) -> p dc t", dc=NDC)[
                    :, fq * 4:fq * 4 + 4, tc * 128:tc * 128 + 128],
                psT[:, 0:512].rearrange("p (a c) -> p a c", a=4)),
                after=[tt], inc="ae")
            P.setw("psT", tev)
            P.setw("d_x1tm", tev)
            x1_t.append(tev)
        if _DEBUG:
            P.dma("s", io["dbgx1"][tc], rs, after=[tml], stream="dbg")

    if _STAGES < 4:
        _finish(P, io, sb, x1_t)
        return

    g_t = []
    for tc in range(8):
        tgm = None
        for dc in range(NDC):
            tgm = P.mm(psE[:, 0:16],
                       sb["x1fm"][:, dc * 1024 + tc * 128: dc * 1024 + tc * 128 + 128],
                       sb["gate_sb"][:, dc * 16:(dc + 1) * 16],
                       dc == 0, dc == NDC - 1,
                       after=x1_t + pre + P.use("psE"), inc=(dc == NDC - 1))
        tmx = P.emit("v", lambda e: e.reduce_max(
            sb["tiny"][:, 4:5], psE[:, 0:16], axis=AX.X), after=[tgm], inc="ve")
        tng = P.emit("v", lambda e: e.tensor_scalar(
            sb["tiny"][:, 5:6], sb["tiny"][:, 4:5], -1.0, None, ALU.mult),
            after=[tmx], inc="ve")
        wg = sb["wg_sb"]
        tex = P.emit("a", lambda e, tc=tc: e.activation(
            wg[:, tc * 16:tc * 16 + 16], psE[:, 0:16], AF.Exp,
            bias=sb["tiny"][:, 5:6], accum_out=sb["ssum"][:, 1:2]),
            after=[tng], inc="ae")
        P.setw("psE", tex)
        trd = P.emit("v", lambda e: e.reciprocal(
            sb["tiny"][:, 7:8], sb["ssum"][:, 1:2]), after=[tex], inc="ve")
        tm1 = P.emit("v", lambda e, tc=tc: e.reduce_max(
            sb["tiny"][:, 8:9], wg[:, tc * 16:tc * 16 + 16], axis=AX.X),
            after=[tex], inc="ve")
        teq = P.emit("v", lambda e, tc=tc: e.tensor_scalar(
            sb["gsc"][:, 0:16], wg[:, tc * 16:tc * 16 + 16],
            sb["tiny"][:, 8:9], None, ALU.is_equal), after=[tm1], inc="ve")
        tp2 = P.emit("v", lambda e, tc=tc: e.scalar_tensor_tensor(
            sb["gsc"][:, 16:32], sb["gsc"][:, 0:16], -2.0,
            wg[:, tc * 16:tc * 16 + 16], ALU.mult, ALU.add),
            after=[teq], inc="ve")
        tm2x = P.emit("v", lambda e: e.reduce_max(
            sb["tiny"][:, 9:10], sb["gsc"][:, 16:32], axis=AX.X),
            after=[tp2], inc="ve")
        tge = P.emit("v", lambda e, tc=tc: e.tensor_scalar(
            sb["gsc"][:, 32:48], wg[:, tc * 16:tc * 16 + 16],
            sb["tiny"][:, 9:10], None, ALU.is_ge), after=[tm2x], inc="ve")
        tw1 = P.emit("v", lambda e, tc=tc: e.tensor_tensor(
            wg[:, tc * 16:tc * 16 + 16], wg[:, tc * 16:tc * 16 + 16],
            sb["gsc"][:, 32:48], ALU.mult), after=[tge], inc="ve")
        tw2 = P.emit("v", lambda e, tc=tc: e.tensor_scalar(
            wg[:, tc * 16:tc * 16 + 16], wg[:, tc * 16:tc * 16 + 16],
            sb["tiny"][:, 7:8], None, ALU.mult), after=[tw1, trd], inc="ve")
        g_t.append(tw2)
    if _DEBUG:
        P.dma("s", io["dbggw"], sb["wg_sb"][:], after=g_t, stream="dbg")

    y_last = []
    for ei in range(NE + 2):
        shared = ei >= NE
        if shared:
            s1, s3, s2 = (io["sw1T"][ei - NE], io["sw3T"][ei - NE],
                          io["sw2T"][ei - NE])
        else:
            s1, s3, s2 = io["w1T"][ei], io["w3T"][ei], io["w2T"][ei]

        h_t = []
        for ic in range(ICN):
            sl = ic % 3
            ps1h = (psC, psE)[ic % 2]
            ps3h = (psD, sb["psTf"])[ic % 2]
            k1 = ("psC", "psE")[ic % 2]
            k3 = ("psD", "psT")[ic % 2]
            t1l = P.dma("g", sb[f"e_w1r{sl}"][:], s1[ic],
                        after=x1_t + P.use(f"e_w1r{sl}"), stream=f"w1r{sl}")
            t3l = P.dma("g", sb[f"e_w3r{sl}"][:], s3[ic],
                        after=x1_t + P.use(f"e_w3r{sl}"), stream=f"w3r{sl}")
            for tg in range(2):
                th1 = None
                for dc in range(NDC):
                    th1 = P.mm(ps1h[:, 0:512],
                               sb[f"e_w1r{sl}"][:, dc * 128:dc * 128 + 128],
                               sb["x1fm"][:, dc * 1024 + tg * 512:
                                          dc * 1024 + tg * 512 + 512],
                               dc == 0, dc == NDC - 1,
                               after=[t1l] + x1_t + P.use(k1),
                               inc=(dc == NDC - 1))
                th3 = None
                for dc in range(NDC):
                    th3 = P.mm(ps3h[:, 0:512],
                               sb[f"e_w3r{sl}"][:, dc * 128:dc * 128 + 128],
                               sb["x1fm"][:, dc * 1024 + tg * 512:
                                          dc * 1024 + tg * 512 + 512],
                               dc == 0, dc == NDC - 1,
                               after=[t3l] + x1_t + P.use(k3),
                               inc=(dc == NDC - 1))
                t1s = (ic % 2) * 512
                tsl_ = P.emit("a", lambda e, ps1h=ps1h, t1s=t1s: e.activation(
                    sb["e_t1"][:, t1s:t1s + 512],
                    ps1h[:, 0:512], AF.Silu),
                    after=[th1] + P.use(f"e_t1{ic % 2}"), inc="ae")
                P.setw(k1, tsl_)
                thm = P.emit("v", lambda e, ic=ic, tg=tg, ps3h=ps3h, t1s=t1s:
                             e.tensor_tensor(
                    sb["e_h"][:, ic * 1024 + tg * 512: ic * 1024 + tg * 512 + 512],
                    sb["e_t1"][:, t1s:t1s + 512], ps3h[:, 0:512], ALU.mult),
                    after=[tsl_, th3] + P.use("e_h"), inc="ve")
                P.setw(k3, thm)
                P.setw(f"e_t1{ic % 2}", thm)
                h_t.append(thm)
            P.setw(f"e_w1r{sl}", h_t[-1])
            P.setw(f"e_w3r{sl}", h_t[-1])
        w2_ld = []
        for ic in range(ICN):
            w2_ld.append(P.dma("g", sb["e_w2"][:, ic * 2048:(ic + 1) * 2048],
                               s2[ic], after=x1_t + P.use("e_w2"), stream="w2"))
        if _DEBUG and ei == 0:
            P.dma("s", io["dbgh"], sb["e_h"][:], after=h_t, stream="dbg")
        y_t = []
        for tg in range(2):
            for tcc in range(4):
                tci = tg * 4 + tcc
                ty4 = [None] * 4
                for ic in range(ICN):
                    for og in range(4):
                        ps = (psA, psB)[og // 2]
                        psl = ps[:, (og % 2) * 512:(og % 2) * 512 + 512]
                        ty4[og] = P.mm(
                            psl,
                            sb["e_h"][:, ic * 1024 + tg * 512 + tcc * 128:
                                      ic * 1024 + tg * 512 + tcc * 128 + 128],
                            sb["e_w2"][:, ic * 2048 + og * 512:
                                       ic * 2048 + og * 512 + 512],
                            ic == 0, ic == ICN - 1,
                            after=h_t + w2_ld +
                            P.use("ps_y0", "ps_y1", "ps_y2", "ps_y3"),
                            inc=(ic == ICN - 1))
                for og in range(4):
                    ps = (psA, psB)[og // 2]
                    psl = ps[:, (og % 2) * 512:(og % 2) * 512 + 512]
                    rsl = sb["res"][:, tci * 2048 + og * 512:
                                    tci * 2048 + og * 512 + 512]
                    if shared:
                        tev = P.emit("v", lambda e, p=psl, r=rsl:
                                     e.scalar_tensor_tensor(
                                         r, p, 1.0, r, ALU.mult, ALU.add),
                                     after=[ty4[og]] + g_t + P.use(f"res{tci}_{og}"),
                                     inc="ve")
                    else:
                        tev = P.emit("v", lambda e, p=psl, r=rsl, tci=tci, ei=ei:
                                     e.scalar_tensor_tensor(
                                         r, p,
                                         sb["wg_sb"][:, tci * 16 + ei:
                                                     tci * 16 + ei + 1],
                                         r, ALU.mult, ALU.add),
                                     after=[ty4[og]] + g_t + P.use(f"res{tci}_{og}"),
                                     inc="ve")
                    P.setw(f"ps_y{og}", tev)
                    P.setw(f"res{tci}_{og}", tev)
                    y_t.append(tev)
        P.setw("e_h", y_t[-1])
        P.setw("e_w2", y_t[-1])
        y_last = y_t

    all_y = [P.war[f"res{t}_{o}"] for t in range(8) for o in range(4)]

    out_t = []
    for tc in range(8):
        rs = sb["res"][:, tc * 2048:(tc + 1) * 2048]
        tm = P.emit("v", lambda e, rs=rs: e.reduce_sum(
            sb["tiny"][:, 10:11], rs, axis=AX.X), after=all_y, inc="ve")
        tm2 = P.emit("v", lambda e: e.tensor_scalar(
            sb["tiny"][:, 11:12], sb["tiny"][:, 10:11], 1.0 / 2048.0, None,
            ALU.mult), after=[tm], inc="ve")
        tsb = P.emit("v", lambda e, rs=rs: e.tensor_scalar(
            rs, rs, sb["tiny"][:, 11:12], None, ALU.subtract),
            after=[tm2], inc="ve")
        tq = P.emit("a", lambda e, rs=rs: e.activation(
            sb["d_sq"][:], rs, AF.Square, accum_out=sb["ssum"][:, 2:3]),
            after=[tsb] + P.use("d_sq"), inc="ae")
        P.setw("d_sq", tq)
        tsd = P.emit("a", lambda e: e.activation(
            sb["tiny"][:, 12:13], sb["ssum"][:, 2:3], AF.Sqrt,
            bias=sb["epsb"][:, 0:1], scale=1.0 / 2048.0),
            after=[tq, teps], inc="ae")
        trc = P.emit("v", lambda e: e.reciprocal(
            sb["tiny"][:, 13:14], sb["tiny"][:, 12:13]), after=[tsd], inc="ve")
        tml = P.emit("v", lambda e, rs=rs: e.tensor_scalar(
            sb["e_oscr"][:], rs, sb["tiny"][:, 13:14], None, ALU.mult),
            after=[trc] + P.use("e_oscr"), inc="ve")
        tst = P.dma("s", io["out"][tc], sb["e_oscr"][:], after=[tml], stream="out")
        P.setw("e_oscr", tst)
        out_t.append(tst)

    _finish(P, io, sb, out_t)


def _finish(P, io, sb, out_t):
    if not out_t:
        allj = [(sn, rec[1]) for sn, rec in P.sem.items() if rec[1] > 0]
        tz = P.emit("v", lambda e: e.memset(sb["e_oscr"][:], 0.0),
                    after=allj, inc="ve")
        out_t = [P.dma("s", io["out"][tc], sb["e_oscr"][:], after=[tz],
                       stream="out") for tc in range(8)]
    P.emit("s", lambda e: e.nop(), after=out_t)


def _last(*toks):
    toks = [t for t in toks if t is not None]
    return toks[-1] if toks else None


def _lastl(toks):
    toks = [t for t in toks if t is not None]
    if not toks:
        return None
    best = {}
    for sn, v in toks:
        best[sn] = max(best.get(sn, 0), v)
    sn, v = max(best.items(), key=lambda kv: kv[1])
    return (sn, v)


_CACHE = {}


def kernel(**inputs):
    maps = _stage(inputs)
    if "nc" not in _CACHE:
        _CACHE["nc"] = _build()
    nc = _CACHE["nc"]
    res = run_bass_kernel_spmd(nc, maps, core_ids=list(range(8)))
    return _assemble(res.results)
